# version 63
# baseline (speedup 1.0000x reference)
import numpy as np
from contextlib import ExitStack

import concourse.bass as bass
import concourse.mybir as mybir
from concourse.bass_utils import run_bass_kernel_spmd

F32 = mybir.dt.float32
BF16 = mybir.dt.bfloat16
I32 = mybir.dt.int32
AF = mybir.ActivationFunctionType
ALU = mybir.AluOpType
AX = mybir.AxisListType

B, T, C = 4, 4096, 1024
DH, NJ = 64, 8
OWN = 2048
QTS, NQT = 512, 4
KCS = 128
E, NE, HID, NHI = 8, 9, 512, 4
EPS = 1e-5

Q0S = [[0, 512, 3072, 3584], [1024, 1536, 2048, 2560]]
LIMS = [[q0 // KCS + 4 for q0 in q0s] for q0s in Q0S]
OWN_SPANS = [[(0, 1024), (3072, 4096)], [(1024, 3072)]]


def own_rows(half):
    return np.concatenate([np.arange(a, b) for a, b in OWN_SPANS[half]])


def own_copy_spans(half):
    out, d = [], 0
    for a, b in OWN_SPANS[half]:
        out.append((d, a, b - a))
        d += b - a
    return out


def own_dma_spans(half):
    out = []
    for (d0, s0, ln) in own_copy_spans(half):
        for k in range(0, ln, 1024):
            out.append((d0 + k, s0 + k, 1024))
    return out


class Rec:

    def __init__(self, eng):
        object.__setattr__(self, "e", eng)
        object.__setattr__(self, "last", None)

    def __getattr__(self, k):
        a = getattr(self.e, k)
        if callable(a):
            def w(*args, **kw):
                r = a(*args, **kw)
                if hasattr(r, "then_inc"):
                    object.__setattr__(self, "last", r)
                return r
            return w
        return a


class St:
    def __init__(self, nc):
        self.nc = nc
        self.t, self.s, self.v, self.g, self.y = (
            Rec(nc.tensor), Rec(nc.scalar), Rec(nc.vector),
            Rec(nc.gpsimd), Rec(nc.sync))
        self.cnt = {}
        self.sems = {}

    def sem(self, stack, name):
        self.sems[name] = stack.enter_context(self.nc.semaphore(name))
        self.cnt[name] = 0

    def inc(self, inst, name, k=1):
        self.cnt[name] += k
        inst.then_inc(self.sems[name], k)
        return self.cnt[name]

    def wait(self, eng, name, val=None):
        if val is None:
            val = self.cnt[name]
        if val > 0:
            eng.wait_ge(self.sems[name], val)

    def dma(self, eng, sem, out, in_):
        return self.inc(eng.dma_start(out, in_), sem, 16)

    def vedge(self):
        mk = self.inc(self.v.last, "dve")
        self.wait(self.v, "dve", mk)
        return mk

    def snap(self):
        return dict(self.cnt)

    def restore(self, snapv):
        self.cnt = dict(snapv)


def build():
    nc = bass.Bass("TRN2", target_bir_lowering=False)
    st = St(nc)
    t, s, v, g, y = st.t, st.s, st.v, st.g, st.y

    xf = nc.dram_tensor("xf", [T, C], F32, kind="ExternalInput")
    xo = nc.dram_tensor("xo", [OWN, C], F32, kind="ExternalInput")
    wq = nc.dram_tensor("wq", [DH, DH], F32, kind="ExternalInput")
    wk = nc.dram_tensor("wk", [DH, DH], F32, kind="ExternalInput")
    wv = nc.dram_tensor("wv", [DH, DH], F32, kind="ExternalInput")
    wo = nc.dram_tensor("wo", [C, C], F32, kind="ExternalInput")
    gwd = nc.dram_tensor("gw", [C, E], F32, kind="ExternalInput")
    w1d = nc.dram_tensor("w1", [E, C, HID], F32, kind="ExternalInput")
    b1d = nc.dram_tensor("b1", [E, HID], F32, kind="ExternalInput")
    w2d = nc.dram_tensor("w2", [E, HID, C], F32, kind="ExternalInput")
    b2d = nc.dram_tensor("b2", [E, C], F32, kind="ExternalInput")
    ws1d = nc.dram_tensor("ws1", [C, HID], F32, kind="ExternalInput")
    bs1d = nc.dram_tensor("bs1", [1, HID], F32, kind="ExternalInput")
    ws2d = nc.dram_tensor("ws2", [HID, C], F32, kind="ExternalInput")
    bs2d = nc.dram_tensor("bs2", [1, C], F32, kind="ExternalInput")
    rsd = nc.dram_tensor("rs", [1, 1], F32, kind="ExternalInput")
    outd = nc.dram_tensor("out", [OWN, C], F32, kind="ExternalOutput")

    avt_d = nc.dram_tensor("avt_scr", [NJ, 128, OWN], BF16, kind="Internal")
    h1g_d = nc.dram_tensor("h1g_scr", [NE, NHI, 128, OWN], BF16, kind="Internal")
    xnt_d = nc.dram_tensor("xnt_scr", [128, 4 * T], BF16, kind="Internal")

    with ExitStack() as big:
        e_ = big.enter_context
        for name in ["din", "dsy", "dsc", "pe", "act", "dve", "pool"]:
            st.sem(big, name)
        SEM = st.sems

        idx = {}
        for name, eng in [("t", t), ("s", s), ("v", v), ("y", y), ("g", g)]:
            idx[name] = eng.compute_val(eng.partition_id() % 2)

        ident_b = e_(nc.sbuf_tensor("ident_b", [128, 128], BF16))
        ident_f = e_(nc.sbuf_tensor("ident_f", [128, 128], F32))
        mask_d = e_(nc.sbuf_tensor("mask_d", [128, 128], BF16))
        ones_b = e_(nc.sbuf_tensor("ones_b", [128, 1], BF16))
        ones_f = e_(nc.sbuf_tensor("ones_f", [1, 128], F32))
        ones_bh = e_(nc.sbuf_tensor("ones_bh", [1, 128], BF16))
        wqt = e_(nc.sbuf_tensor("wqt", [128, 128], BF16))
        wkt = e_(nc.sbuf_tensor("wkt", [128, 128], BF16))
        wvt = e_(nc.sbuf_tensor("wvt", [128, 128], BF16))
        wot_sb = e_(nc.sbuf_tensor("wot_sb", [128, NJ * 1024], BF16))
        gw_sb = e_(nc.sbuf_tensor("gw_sb", [128, NJ * E], BF16))
        b1c = e_(nc.sbuf_tensor("b1c", [128, NE * NHI], F32))
        rs_sb = e_(nc.sbuf_tensor("rs_sb", [1, 1], F32))
        rs_b = e_(nc.sbuf_tensor("rs_b", [128, 1], F32))
        negm = e_(nc.sbuf_tensor("negm", [128, 1], F32))
        m11 = e_(nc.sbuf_tensor("m11", [1, 4], F32))
        rstd_sb = e_(nc.sbuf_tensor("rstd_sb", [128, 32], F32))
        xnt_own = e_(nc.sbuf_tensor("xnt_own", [128, NJ * OWN], BF16))
        gtb_sb = e_(nc.sbuf_tensor("gtb_sb", [9, OWN], BF16))
        sel8 = e_(nc.sbuf_tensor("sel8", [8, E * 128], BF16))
        g0_sb = e_(nc.sbuf_tensor("g0_sb", [128, 16 * 8], F32))
        gt1 = e_(nc.sbuf_tensor("gt1", [128, 128], F32))
        gt2 = e_(nc.sbuf_tensor("gt2", [128, 128], F32))
        gsc = e_(nc.sbuf_tensor("gsc", [128, 5 * 16], F32))

        xg_stack = ExitStack()
        xnt_g = xg_stack.enter_context(nc.sbuf_tensor("xnt_g", [128, 4 * T], BF16))

        with ExitStack() as ph:
            pe_ = ph.enter_context
            xrow_b = pe_(nc.sbuf_tensor("xrow_b", [128, 4 * 4096], F32))
            xn_pb = [pe_(nc.sbuf_tensor(f"xn_pb{i}", [128, C], BF16)) for i in range(4)]
            xnt_hb = [pe_(nc.sbuf_tensor(f"xnt_hb{i}", [128, 2048], BF16)) for i in range(2)]
            stt_sb = pe_(nc.sbuf_tensor("stt_sb", [128, 2 * 48], F32))
            agg_sb = pe_(nc.sbuf_tensor("agg_sb", [128, 2 * 8], F32))
            iota_i = pe_(nc.sbuf_tensor("iota_i", [128, 512], I32))
            iota8 = pe_(nc.sbuf_tensor("iota8", [8, E * 128], I32))
            selt = pe_(nc.sbuf_tensor("selt", [8, E * 128], BF16))
            wtmp = pe_(nc.sbuf_tensor("wtmp", [128, 3 * DH], BF16))
            var_sb = pe_(nc.sbuf_tensor("var_sb", [128, 32], F32))
            ps_su = pe_(nc.psum_tensor("ps_su", [128, 512], BF16))
            ps_sf = pe_(nc.psum_tensor("ps_sf", [128, 512], F32))
            ps_x = pe_(nc.psum_tensor("ps_x", [128, 2048], F32))
            ps_xb = ps_x.bitcast(BF16)

            st.inc(g.iota(iota_i[:, :], [[1, 512]], channel_multiplier=-1), "pool")
            st.inc(g.iota(iota8[:, :], [[1, E * 128]], channel_multiplier=-128), "pool")
            iota_mk = st.cnt["pool"]
            st.dma(g, "din", wtmp[0:DH, 0:DH], wq[:, :])
            st.dma(g, "din", wtmp[0:DH, DH:2 * DH], wk[:, :])
            st.dma(g, "din", wtmp[0:DH, 2 * DH:3 * DH], wv[:, :])
            wqkv_ld = st.cnt["din"]
            st.wait(g, "din", wqkv_ld)
            for j in range(NJ):
                st.dma(g, "din", gw_sb[:, j * E:(j + 1) * E],
                       gwd[j * 128:(j + 1) * 128, :])
            for ei in range(E):
                for hi in range(NHI):
                    st.dma(g, "din", b1c[:, ei * NHI + hi:ei * NHI + hi + 1],
                           b1d[ei, hi * 128:(hi + 1) * 128].rearrange("(h o) -> h o", o=1))
            for hi in range(NHI):
                st.dma(g, "din", b1c[:, E * NHI + hi:E * NHI + hi + 1],
                       bs1d[0, hi * 128:(hi + 1) * 128].rearrange("(h o) -> h o", o=1))
            st.dma(g, "din", rs_sb[:, :], rsd[:, :])
            small_loads = st.cnt["din"]

            v.memset(ones_b[:, :], 1.0)
            v.memset(ones_f[:, :], 1.0)
            v.memset(ones_bh[:, :], 1.0)
            st.wait(v, "pool", iota_mk)
            v.tensor_single_scalar(ident_b[:, :], iota_i[:, 0:128], 0.0, ALU.is_equal)
            v.tensor_single_scalar(ident_f[:, :], iota_i[:, 0:128], 0.0, ALU.is_equal)
            v.tensor_single_scalar(mask_d[:, :], iota_i[:, 0:128], 0.0, ALU.is_ge)
            v.tensor_single_scalar(sel8[:, :], iota8[:, :], 0.0, ALU.is_ge)
            v.tensor_single_scalar(selt[:, :], iota8[:, :], 127.0, ALU.is_le)
            st.vedge()
            v.tensor_mul(sel8[:, :], sel8[:, :], selt[:, :])
            mk_ident = st.inc(v.last, "dve")
            sel8_mk = mk_ident

            st.wait(t, "din", wqkv_ld)
            st.wait(t, "dve", mk_ident)
            for i, wt_dst in enumerate([wqt, wkt, wvt]):
                v.memset(wt_dst[:, :], 0.0)
                st.vedge()
                t.transpose(ps_su[0:DH, 0:64], wtmp[0:DH, i * DH:(i + 1) * DH],
                            ident_b[0:DH, 0:DH])
                mk = st.inc(t.last, "pe")
                st.wait(v, "pe", mk)
                v.tensor_copy(wt_dst[0:DH, 0:DH], ps_su[0:DH, 0:64])
                v.tensor_copy(wt_dst[DH:128, DH:128], ps_su[0:DH, 0:64])
                mk2 = st.inc(v.last, "dve")
                st.wait(t, "dve", mk2)

            dma_mk, sw_sq, sw_math, sw_rstd = {}, {}, {}, {}
            sw_aff, sw_tr, sw_evs, sw_sp = {}, {}, {}, {}
            for it in range(11):
                if it < 8:
                    b = it
                    if b >= 4:
                        y.wait_ge(SEM["dve"], sw_aff[b - 4])
                    for k in range(4):
                        tt = 4 * b + k
                        qe, qs = (y, "dsy")
                        dma_mk[(b, k)] = (qs, st.dma(
                            qe, qs,
                            xrow_b[:, (b % 4) * 4096 + k * 1024:
                                   (b % 4) * 4096 + (k + 1) * 1024],
                            xf[tt * 128:(tt + 1) * 128, :]))
                    for k in range(4):
                        xs = xrow_b[:, (b % 4) * 4096 + k * 1024:
                                    (b % 4) * 4096 + (k + 1) * 1024]
                        st.wait(v, dma_mk[(b, k)][0], dma_mk[(b, k)][1])
                        for hf in range(2):
                            v.bn_stats(
                                stt_sb[:, (b % 2) * 48 + k * 12 + hf * 6:
                                       (b % 2) * 48 + k * 12 + (hf + 1) * 6].rearrange(
                                    "p (j s) -> p j s", s=6),
                                xs[:, hf * 512:(hf + 1) * 512].rearrange(
                                    "p (j c) -> p j c", j=1))
                        sw_sq[b] = st.inc(v.last, "dve")
                if 1 <= it <= 8:
                    b = it - 1
                    c4 = slice(b * 4, (b + 1) * 4)
                    st.wait(v, "dve", sw_sq[b])
                    for k in range(4):
                        v.bn_aggr(
                            agg_sb[:, (b % 2) * 8 + k * 2:(b % 2) * 8 + k * 2 + 2],
                            stt_sb[:, (b % 2) * 48 + k * 12:(b % 2) * 48 + (k + 1) * 12])
                    st.vedge()
                    v.tensor_scalar(
                        var_sb[:, c4],
                        agg_sb[:, (b % 2) * 8:(b % 2) * 8 + 8].rearrange(
                            "p (j s) -> p j s", s=2)[:, :, 1:2].rearrange(
                            "p j o -> p (j o)"),
                        EPS, None, ALU.add)
                    sw_math[b] = st.inc(v.last, "dve")
                    st.wait(s, "dve", sw_math[b])
                    s.activation(var_sb[:, c4], var_sb[:, c4], AF.Ln)
                    mk = st.inc(s.last, "act")
                    st.wait(s, "act", mk)
                    s.activation(rstd_sb[:, c4], var_sb[:, c4],
                                 AF.Exp, scale=-0.5)
                    sw_rstd[b] = st.inc(s.last, "act")
                if 2 <= it <= 9:
                    b = it - 2
                    for k in range(4):
                        tt = 4 * b + k
                        slot = k * 1024
                        xs = xrow_b[:, (b % 4) * 4096 + k * 1024:
                                    (b % 4) * 4096 + (k + 1) * 1024]
                        st.wait(v, "act", sw_rstd[b])
                        if tt >= 4:
                            st.wait(v, "pe", sw_tr[tt - 4])
                        v.tensor_scalar(xn_pb[tt % 4][:, :], xs,
                                        agg_sb[:, (b % 2) * 8 + k * 2:
                                               (b % 2) * 8 + k * 2 + 1],
                                        rstd_sb[:, tt:tt + 1],
                                        ALU.subtract, ALU.mult)
                        sw_aff[b] = st.inc(v.last, "dve")
                        st.wait(t, "dve", sw_aff[b])
                        if b >= 1:
                            st.wait(t, "act", sw_evs[(b - 1, k)])
                        for ci in range(8):
                            t.transpose(
                                ps_xb[:, slot + ci * 128: slot + (ci + 1) * 128],
                                xn_pb[tt % 4][:, ci * 128:(ci + 1) * 128],
                                ident_b[:, :])
                        sw_tr[tt] = st.inc(t.last, "pe")
                        st.wait(s, "pe", sw_tr[tt])
                        s.copy(
                            xnt_g[:, :].rearrange("p (c n) -> p c n", c=4)[:, :, tt * 128:(tt + 1) * 128],
                            ps_xb[:, slot:slot + 512].rearrange("p (c n) -> p c n", c=4))
                        if b >= 2 and k == 0:
                            st.wait(s, "din", sw_sp[b - 2])
                        s.copy(
                            xnt_hb[b % 2][:, :].rearrange("p (c n) -> p c n", c=4)[:, :, k * 128:(k + 1) * 128],
                            ps_xb[:, slot + 512:slot + 1024].rearrange("p (c n) -> p c n", c=4))
                        sw_evs[(b, k)] = st.inc(s.last, "act")
                        if k == 3:
                            st.wait(g, "act", sw_evs[(b, 3)])
                            sw_sp[b] = st.dma(
                                g, "din",
                                xnt_d[:, :].rearrange("p (c n) -> p c n", c=4)[:, :, b * 512:(b + 1) * 512],
                                xnt_hb[b % 2][:, :].rearrange("p (c n) -> p c n", c=4))

            st.wait(v, "act", sw_evs[(7, 3)])
            st.wait(y, "din", sw_sp[7])
            snapv = st.snap()
            ends = []
            for bi in nc.Switch(engines=[v.e, y.e], index=[idx["v"], idx["y"]], n=2):
                st.restore(snapv)
                for jl in range(4):
                    for (d0, s0, ln) in own_copy_spans(bi):
                        v.tensor_copy(
                            xnt_own[:, jl * OWN + d0: jl * OWN + d0 + ln],
                            xnt_g[:, jl * T + s0: jl * T + s0 + ln])
                st.inc(v.last, "dve")
                for jl in range(4, 8):
                    for (d0, s0, ln) in own_dma_spans(bi):
                        st.dma(y, "dsy", xnt_own[:, jl * OWN + d0: jl * OWN + d0 + ln],
                               xnt_d[:, (jl - 4) * T + s0:(jl - 4) * T + s0 + ln])
                ends.append(st.snap())
            assert ends[0] == ends[1]
            st.restore(ends[0])
            own_gather_mk = st.cnt["dsy"]

            st.wait(t, "din", small_loads)
            t.matmul(ps_sf[:, 0:1], ones_f[0:1, :], rs_sb[0:1, 0:1],
                     start=True, stop=True)
            mk = st.inc(t.last, "pe")
            st.wait(v, "pe", mk)
            v.tensor_copy(rs_b[:, :], ps_sf[:, 0:1])
            st.inc(v.last, "dve")


            nc.all_engine_barrier()

        nc.all_engine_barrier()

        for grp in range(2):
            with ExitStack() as ph:
                pe_ = ph.enter_context
                kt_sb = pe_(nc.sbuf_tensor(f"kt_sb{grp}", [128, 4 * T], BF16))
                v_sb = pe_(nc.sbuf_tensor(f"v_sb{grp}", [128, 32 * 8 * 65], BF16))
                qt_sb = pe_(nc.sbuf_tensor(f"qt_sb{grp}", [128, 4 * OWN], BF16))
                ps_sc = pe_(nc.psum_tensor(f"ps_sc{grp}", [128, 2048], F32))
                ps_av_s = [pe_(nc.psum_tensor(f"ps_av{grp}_{i}", [128, 1024], F32))
                           for i in range(2)]
                pha = ph.enter_context(ExitStack())
                sqm = pha.enter_context(
                    nc.sbuf_tensor(f"sqm{grp}", [128, 6 * 2048], BF16))

                st.wait(t, "dve", st.cnt["dve"])
                if grp == 1:
                    st.wait(t, "din", xnt_reload_mk)
                    st.wait(y, "din", wo_ld)
                    for a in range(8):
                        for b in range(8):
                            st.inc(y.dma_start_transpose(
                                wot_sb[:, b * 1024 + a * 128: b * 1024 + (a + 1) * 128],
                                xnt_d[:, a * 1024 + b * 128: a * 1024 + (b + 1) * 128]),
                                "dsy", 16)
                kq_work = []
                for jl in range(4):
                    for ti in range(8):
                        kq_work.append(("k", jl, ti))
                for jl in range(4):
                    for ti in range(4):
                        kq_work.append(("q", jl, ti))
                mm_mks, ev_mks, ev_eng = [], [], []

                def kq_mm(i):
                    kind, jl, ti = kq_work[i]
                    sl = i % 4
                    if grp == 0 and i == 32:
                        st.wait(t, "dsy", own_gather_mk)
                    if i >= 4:
                        st.wait(t, "dve" if ev_eng[i - 4] == "v" else "act",
                                ev_mks[i - 4])
                    if kind == "k":
                        mm = t.matmul(ps_sc[:, sl * 512:(sl + 1) * 512],
                                      wkt[:, :],
                                      xnt_g[:, jl * T + ti * 512:
                                            jl * T + (ti + 1) * 512],
                                      start=True, stop=True)
                    else:
                        jglob = grp * 4 + jl
                        mm = t.matmul(ps_sc[:, sl * 512:(sl + 1) * 512],
                                      wqt[:, :],
                                      xnt_own[:, jglob * OWN + ti * 512:
                                              jglob * OWN + (ti + 1) * 512],
                                      start=True, stop=True)
                    mm_mks.append(st.inc(mm, "pe"))

                def kq_ev(i):
                    kind, jl, ti = kq_work[i]
                    sl = i % 4
                    dst = (kt_sb[:, jl * T + ti * 512: jl * T + (ti + 1) * 512]
                           if kind == "k" else
                           qt_sb[:, jl * OWN + ti * 512: jl * OWN + (ti + 1) * 512])
                    if i % 2 == 0:
                        st.wait(v, "pe", mm_mks[i])
                        v.tensor_copy(dst, ps_sc[:, sl * 512:(sl + 1) * 512])
                        ev_mks.append(st.inc(v.last, "dve"))
                        ev_eng.append("v")
                    else:
                        st.wait(s, "pe", mm_mks[i])
                        s.copy(dst, ps_sc[:, sl * 512:(sl + 1) * 512])
                        ev_mks.append(st.inc(s.last, "act"))
                        ev_eng.append("s")

                for i in range(len(kq_work)):
                    kq_mm(i)
                    if i >= 1:
                        kq_ev(i - 1)
                kq_ev(len(kq_work) - 1)

                waves = ([("q", jl) for jl in range(4)]
                         + [("k0", jl) for jl in range(4)]
                         + [("k2", jl) for jl in range(4)])

                def wave_src(w):
                    kind, jl = waves[w]
                    if kind == "q":
                        return qt_sb[:, jl * OWN: jl * OWN + 2048]
                    if kind == "k0":
                        return kt_sb[:, jl * T: jl * T + 2048]
                    return kt_sb[:, jl * T + 2048: jl * T + 4096]

                sq_mk, nm_mk = [], []

                def emit_square(w):
                    sl = (w % 6) * 2048
                    s.activation(sqm[:, sl:sl + 2048], wave_src(w), AF.Square)
                    sq_mk.append(st.inc(s.last, "act"))

                st.wait(s, "dve", st.cnt["dve"])
                for w in range(6):
                    emit_square(w)

                v.memset(v_sb[:, :].rearrange("p (a b) -> p a b", b=65)[:, :, 64:65], 1.0)
                st.inc(v.last, "dve")
                st.wait(t, "dve", st.cnt["dve"])
                vps = ps_av_s
                vmm_mks, vev_mks, vev_eng = [], [], []

                def v_mm(i):
                    jl, tt = divmod(i, 32)
                    ps = vps[i % 2]
                    if i >= 2:
                        st.wait(t, "dve" if vev_eng[i - 2] == "v" else "act",
                                vev_mks[i - 2])
                    mm = t.matmul(
                        ps[:, 0:128],
                        xnt_g[:, jl * T + tt * 128: jl * T + (tt + 1) * 128],
                        wvt[:, :],
                        start=True, stop=True)
                    vmm_mks.append(st.inc(mm, "pe"))

                def v_ev(i):
                    jl, tt = divmod(i, 32)
                    ps = vps[i % 2]
                    st.wait(v, "pe", vmm_mks[i])
                    v.tensor_copy(
                        v_sb[:, tt * 520 + (jl * 2) * 65:
                             tt * 520 + (jl * 2) * 65 + 130].rearrange(
                            "p (h n) -> p h n", n=65)[:, :, 0:64],
                        ps[:, 0:128].rearrange("p (h n) -> p h n", n=64))
                    vev_mks.append(st.inc(v.last, "dve"))
                    vev_eng.append("v")

                for i in range(128):
                    v_mm(i)
                    if i >= 1:
                        v_ev(i - 1)
                v_ev(127)

                st.wait(t, "dve", st.cnt["dve"])
                st.wait(t, "act", st.cnt["act"])
                for w in range(12):
                    sl = (w % 6) * 2048
                    jl = w % 4
                    st.wait(t, "act", sq_mk[w])
                    for ti in range(4):
                        mm = t.matmul(ps_sc[0:1, ti * 512:(ti + 1) * 512],
                                      ones_b[:, 0:1],
                                      sqm[:, sl + ti * 512: sl + (ti + 1) * 512],
                                      start=(jl == 0), stop=(jl == 3))
                    nm_mk.append(st.inc(mm, "pe"))
                    if w + 6 < 12:
                        st.wait(s, "pe", nm_mk[w])
                        emit_square(w + 6)
                    if jl == 3:
                        st.wait(v, "pe", nm_mk[w])
                        v.reduce_max(m11[0:1, w // 4: w // 4 + 1],
                                     ps_sc[0:1, 0:2048], AX.X)
                        mk = st.inc(v.last, "dve")
                        st.wait(t, "dve", mk)
                v.tensor_max(m11[0:1, 1:2], m11[0:1, 1:2], m11[0:1, 2:3])
                st.vedge()
                v.tensor_mul(m11[0:1, 0:1], m11[0:1, 0:1], m11[0:1, 1:2])
                mk = st.inc(v.last, "dve")
                st.wait(s, "dve", mk)
                s.activation(m11[0:1, 3:4], m11[0:1, 0:1], AF.Ln)
                mk = st.inc(s.last, "act")
                st.wait(s, "act", mk)
                s.activation(m11[0:1, 3:4], m11[0:1, 3:4], AF.Exp, scale=0.5)
                mk = st.inc(s.last, "act")
                st.wait(v, "act", mk)
                v.tensor_scalar(m11[0:1, 3:4], m11[0:1, 3:4], -0.1275, None, ALU.mult)
                mk = st.inc(v.last, "dve")
                st.wait(t, "dve", mk)
                t.matmul(ps_av_s[0][:, 0:1], ones_f[0:1, :], m11[0:1, 3:4],
                         start=True, stop=True)
                mk = st.inc(t.last, "pe")
                st.wait(v, "pe", mk)
                v.tensor_copy(negm[:, :], ps_av_s[0][:, 0:1])
                st.inc(v.last, "dve")

                nc.all_engine_barrier()
                if grp == 0:
                    xnt_reload_mk = st.dma(g, "din", xnt_g[:, :], xnt_d[:, :])
                    st.wait(g, "din", xnt_reload_mk)
                    for ic in range(8):
                        wo_ld = st.dma(
                            g, "din", xnt_d[:, ic * 1024:(ic + 1) * 1024],
                            wo[ic * 128:(ic + 1) * 128, :])

                pha.close()
                phb = ph.enter_context(ExitStack())
                pb_ = phb.enter_context
                pt_pb = [pb_(nc.sbuf_tensor(f"pt{grp}_{i}", [128, 1024], BF16)) for i in range(4)]
                av_pb = [pb_(nc.sbuf_tensor(f"av{grp}_{i}", [128, 512], BF16)) for i in range(4)]
                rz_sb = pb_(nc.sbuf_tensor(f"rz{grp}", [1, 1024], BF16))
                rzl_sb = pb_(nc.sbuf_tensor(f"rzl{grp}", [1, 1024], F32))
                rzb_sb = pb_(nc.sbuf_tensor(f"rzb{grp}", [64, 1024], F32))

                snapv = st.snap()
                ends = []
                for bi in nc.Switch(engines=[t.e, s.e, v.e, y.e],
                                    index=[idx["t"], idx["s"], idx["v"], idx["y"]],
                                    n=2):
                    st.restore(snapv)
                    q0s, lims = Q0S[bi], LIMS[bi]
                    slot_free = {0: 0, 1: 0}
                    prev_av4 = [0, 0, 0, 0]
                    prev_exmk = [0, 0]
                    dma_mks = {}
                    pending = [None]
                    pq_list = [(pl, qt) for pl in range(4) for qt in range(NQT)]

                    def pq_ctx(pqi):
                        pl, qt = pq_list[pqi]
                        q0, L = q0s[qt], lims[qt]
                        order = list(range(L - 4, L)) + list(range(0, L - 4))
                        return pl, qt, q0, L, order

                    def emit_scores(pqi, j, exwait):
                        pl, qt, q0, L, order = pq_ctx(pqi)
                        kc = pq_ctx(pqi)[4][j]
                        scp = j % 2
                        coff = kc * 128 - q0
                        sk = coff if coff >= 256 else 0
                        st.wait(t, "act", exwait)
                        for h in range(2):
                            mm = t.matmul(
                                ps_sc[:, scp * 1024 + h * 512 + sk:
                                      scp * 1024 + (h + 1) * 512],
                                kt_sb[h * 64:(h + 1) * 64,
                                      pl * T + kc * 128: pl * T + (kc + 1) * 128],
                                qt_sb[h * 64:(h + 1) * 64,
                                      pl * OWN + qt * 512 + sk:
                                      pl * OWN + (qt + 1) * 512],
                                start=True, stop=True,
                                tile_position=(64 * h, 0))
                        return st.inc(mm, "pe")

                    def run_pq(pqi, sc_pre):
                        pl, qt, q0, L, order = pq_ctx(pqi)
                        jglob = grp * 4 + pl
                        av_mk, ex_mk, sc_mk, msk_mk = [], [], [], {}
                        if sc_pre is not None:
                            sc_mk.append(sc_pre)

                        def emit_exp_mask(j):
                            kc = order[j]
                            p3 = j % 4
                            scp = j % 2
                            st.wait(s, "pe", sc_mk[j])
                            st.wait(s, "pe",
                                    av_mk[j - 4] if j >= 4 else prev_av4[j])
                            coff = kc * 128 - q0
                            if coff >= 256:
                                for h in range(2):
                                    ai = s.activation(
                                        pt_pb[p3][:, h * 512 + coff:(h + 1) * 512],
                                        ps_sc[:, scp * 1024 + h * 512 + coff:
                                              scp * 1024 + (h + 1) * 512],
                                        AF.Exp, bias=negm[:, 0:1], scale=0.125)
                                ex_mk.append(st.inc(ai, "act"))
                            else:
                                ai = s.activation(pt_pb[p3][:, :],
                                                  ps_sc[:, scp * 1024: scp * 1024 + 1024],
                                                  AF.Exp, bias=negm[:, 0:1], scale=0.125)
                                ex_mk.append(st.inc(ai, "act"))
                            if coff >= 0:
                                st.wait(v, "act", ex_mk[j])
                                for h in range(2):
                                    off = h * 512 + coff
                                    v.tensor_mul(pt_pb[p3][:, off:off + 128],
                                                 pt_pb[p3][:, off:off + 128],
                                                 mask_d[:, :])
                                msk_mk[j] = st.inc(v.last, "dve")

                        def emit_av(j):
                            kc = order[j]
                            p3 = j % 4
                            coff = max(0, kc * 128 - q0)
                            if j in msk_mk:
                                st.wait(t, "dve", msk_mk[j])
                            else:
                                st.wait(t, "act", ex_mk[j])
                            if j == 0 and slot_free[pqi % 2]:
                                st.wait(t, "dve", slot_free[pqi % 2])
                            for h in range(2):
                                hloc = pl * 2 + h
                                mm = t.matmul(
                                    ps_av_s[pqi % 2][0:65, h * 512 + coff:(h + 1) * 512],
                                    v_sb[:, kc * 520 + hloc * 65:
                                         kc * 520 + hloc * 65 + 65],
                                    pt_pb[p3][:, h * 512 + coff:(h + 1) * 512],
                                    start=(j == 0), stop=(j == L - 1))
                            av_mk.append(st.inc(mm, "pe"))

                        for j in range(L):
                            if not (j == 0 and sc_pre is not None):
                                sc_mk.append(emit_scores(
                                    pqi, j,
                                    ex_mk[j - 2] if j >= 2 else prev_exmk[j % 2]))
                            if j >= 1:
                                emit_exp_mask(j - 1)
                            if j == 1 and pending[0] is not None:
                                pending[0][0]()
                            if j == 2 and pending[0] is not None:
                                pending[0][1]()
                                pending[0] = None
                            if j >= 2:
                                emit_av(j - 2)
                        emit_exp_mask(L - 1)
                        nxt = None
                        if pqi + 1 < len(pq_list):
                            nxt = emit_scores(pqi + 1, 0,
                                              ex_mk[L - 2] if L >= 2 else 0)
                        emit_av(L - 2)
                        emit_av(L - 1)
                        for _i4 in range(4):
                            prev_av4[_i4] = av_mk[L - 4 + ((_i4 - (L - 4)) % 4)]
                        prev_exmk[0], prev_exmk[1] = (
                            ex_mk[L - 2] if (L - 2) % 2 == 0 else ex_mk[L - 1],
                            ex_mk[L - 1] if (L - 1) % 2 == 1 else ex_mk[L - 2])

                        def mk_pending(pq2=pqi, jglob=jglob, qt=qt,
                                       last_av=av_mk[-1]):
                            state = {}
                            pav = ps_av_s[pq2 % 2]

                            def run_s():
                                st.wait(s, "pe", last_av)
                                s.activation(rzl_sb[0:1, :], pav[64:65, 0:1024],
                                             AF.Ln)
                                mkl = st.inc(s.last, "act")
                                st.wait(s, "act", mkl)
                                s.activation(rz_sb[0:1, :], rzl_sb[0:1, :],
                                             AF.Exp, scale=-1.0)
                                state["rz"] = st.inc(s.last, "act")

                            def run_tv():
                                par2 = pq2 % 4
                                st.wait(t, "act", state["rz"])
                                for h in range(2):
                                    mm = t.matmul(pav[64:128, h * 512:(h + 1) * 512],
                                                  ones_bh[0:1, 0:64],
                                                  rz_sb[0:1, h * 512:(h + 1) * 512],
                                                  start=True, stop=True,
                                                  tile_position=(0, 64))
                                bc_mk = st.inc(mm, "pe")
                                st.wait(v, "pe", bc_mk)
                                if pq2 >= 4:
                                    st.wait(v, "dsy", dma_mks[pq2 - 4])
                                v.tensor_copy(rzb_sb[:, :], pav[64:128, 0:1024])
                                st.vedge()
                                for h in range(2):
                                    v.scalar_tensor_tensor(
                                        av_pb[par2][h * 64:(h + 1) * 64, :],
                                        pav[0:64, h * 512:(h + 1) * 512], 1.0,
                                        rzb_sb[0:64, h * 512:(h + 1) * 512],
                                        ALU.mult, ALU.mult)
                                slot_free[pq2 % 2] = st.inc(v.last, "dve")
                                st.wait(y, "dve", slot_free[pq2 % 2])
                                dma_mks[pq2] = st.dma(
                                    y, "dsy",
                                    avt_d[jglob, :, qt * 512:(qt + 1) * 512],
                                    av_pb[par2][:, :])
                            return (run_s, run_tv)

                        pending[0] = mk_pending()
                        return nxt

                    sc_next = None
                    for pqi in range(len(pq_list)):
                        sc_next = run_pq(pqi, sc_next)
                    pending[0][0]()
                    pending[0][1]()
                    ends.append(st.snap())
                assert ends[0] == ends[1], (ends[0], ends[1])
                st.restore(ends[0])

                nc.all_engine_barrier()

        attn_scr_mk = st.cnt["dsy"]
        xg_stack.close()

        with ExitStack() as ph:
            pe_ = ph.enter_context
            ps_g = pe_(nc.psum_tensor("ps_g", [128, 128], F32))
            ps_gt = pe_(nc.psum_tensor("ps_gt", [128, 2048], F32))
            st.wait(t, "din", small_loads)
            for ts_ in range(16):
                for j in range(NJ):
                    mm = t.matmul(ps_g[:, ts_ * 8:(ts_ + 1) * 8],
                                  xnt_own[:, j * OWN + ts_ * 128: j * OWN + (ts_ + 1) * 128],
                                  gw_sb[:, j * E:(j + 1) * E],
                                  start=(j == 0), stop=(j == NJ - 1))
            mk = st.inc(mm, "pe")
            st.wait(s, "pe", mk)
            s.activation(g0_sb[:, :], ps_g[:, :], AF.Sigmoid)
            gmk_sig = st.inc(s.last, "act")
            st.wait(v, "act", gmk_sig)
            g3 = g0_sb[:, :].rearrange("p (t e) -> p t e", e=E)
            gt1_3 = gt1[:, :].rearrange("p (t e) -> p t e", e=E)
            gt2_3 = gt2[:, :].rearrange("p (t e) -> p t e", e=E)

            def col(i):
                return gsc[:, i * 16:(i + 1) * 16]

            def colb(i):
                return gsc[:, i * 16:(i + 1) * 16].rearrange(
                    "p (t o) -> p t o", o=1).broadcast_to([128, 16, E])

            def col3(i):
                return gsc[:, i * 16:(i + 1) * 16].rearrange("p (t o) -> p t o", o=1)

            v.tensor_scalar(g0_sb[:, :], g0_sb[:, :], rs_b[:, 0:1], None, ALU.mult)
            st.vedge()
            v.reduce_max(col3(0), g3, AX.X)
            st.vedge()
            v.tensor_tensor(gt1_3, g3, colb(0), ALU.is_ge)
            st.vedge()
            v.scalar_tensor_tensor(gt2[:, :], gt1[:, :], -1e30, g0_sb[:, :],
                                   ALU.mult, ALU.add)
            st.vedge()
            v.reduce_max(col3(1), gt2_3, AX.X)
            st.vedge()
            v.tensor_tensor(gt2_3, gt2_3, colb(1), ALU.is_ge)
            st.vedge()
            v.tensor_add(col(2), col(0), col(1))
            st.vedge()
            v.reciprocal(col(2), col(2))
            st.vedge()
            v.tensor_mul(col(3), col(0), col(2))
            v.tensor_mul(col(4), col(1), col(2))
            st.vedge()
            v.tensor_tensor(gt1_3, gt1_3, colb(3), ALU.mult)
            v.tensor_tensor(gt2_3, gt2_3, colb(4), ALU.mult)
            st.vedge()
            v.tensor_add(g0_sb[:, :], gt1[:, :], gt2[:, :])
            gv_all = st.inc(v.last, "dve")
            st.wait(t, "dve", gv_all)
            for ts_ in range(16):
                t.transpose(ps_gt[0:8, ts_ * 128:(ts_ + 1) * 128],
                            g0_sb[:, ts_ * 8:(ts_ + 1) * 8], ident_f[:, :])
            mk = st.inc(t.last, "pe")
            st.wait(v, "pe", mk)
            v.memset(gtb_sb[0:9, :], 1.0)
            st.vedge()
            v.tensor_copy(gtb_sb[0:8, :], ps_gt[0:8, 0:2048])
            st.inc(v.last, "dve")

            nc.all_engine_barrier()

        ffn_outer = big.enter_context(ExitStack())
        fo_ = ffn_outer.enter_context
        w2_sb = fo_(nc.sbuf_tensor("w2_sb", [128, 40 * 1024], BF16))
        b2_sb = fo_(nc.sbuf_tensor("b2_sb", [E + 1, C], BF16))
        w2_thunks = []
        w2_thunks.append(lambda: st.dma(g, "din", b2_sb[0:E, :], b2d[:, :]))
        w2_thunks.append(lambda: st.dma(g, "din", b2_sb[E:E + 1, :], bs2d[:, :]))
        for _ei in range(NE):
            _src = w2d[_ei] if _ei < E else ws2d
            for _hi in range(NHI):
                def _th(src=_src, hi=_hi, ch=_ei * 4 + _hi):
                    st.dma(g, "din", w2_sb[:, ch * 1024:(ch + 1) * 1024],
                           src[hi * 128:(hi + 1) * 128, :])
                w2_thunks.append(_th)

        with ExitStack() as ph:
            pe_ = ph.enter_context
            w1_pb = [pe_(nc.sbuf_tensor(f"w1_pb{i}", [128, 8 * 512], BF16)) for i in range(2)]
            h1_pb = [pe_(nc.sbuf_tensor(f"h1_pb{i}", [128, NHI * OWN], BF16)) for i in range(2)]
            ps_h = pe_(nc.psum_tensor("ps_h", [128, 2048], F32))
            ps_gb = pe_(nc.psum_tensor("ps_gb", [128, 2048], F32))

            w1mk = []

            def load_w1(ei):
                par = ei % 2
                src = w1d[ei] if ei < E else ws1d
                for ci in range(8):
                    st.dma(g, "din", w1_pb[par][:, ci * 512:(ci + 1) * 512],
                           src[ci * 128:(ci + 1) * 128, :])
                w1mk.append(st.cnt["din"])
                for _ in range(5):
                    if w2_thunks:
                        w2_thunks.pop(0)()

            load_w1(0)
            hdma, marks = {}, {}
            gb_free = []
            for ei in range(NE):
                par = ei % 2
                if ei + 1 < NE:
                    if ei >= 1:
                        g.wait_ge(SEM["pe"], marks[("mmlast", ei - 1)])
                    load_w1(ei + 1)
                st.wait(t, "din", w1mk[ei])
                for hi in range(NHI):
                    for tp in range(2):
                        kkp = hi * 2 + tp
                        bank = (kkp % 2) * 2
                        key = (("geluP", ei, kkp - 2) if kkp >= 2
                               else ("geluP", ei - 1, kkp + 6))
                        if key in marks:
                            st.wait(t, "act", marks[key])
                        for ci in range(8):
                            for tx in range(2):
                                ti = 2 * tp + tx
                                mm = t.matmul(
                                    ps_h[:, (bank + tx) * 512:(bank + tx + 1) * 512],
                                    w1_pb[par][:, ci * 512 + hi * 128:
                                               ci * 512 + (hi + 1) * 128],
                                    xnt_own[:, ci * OWN + ti * 512:
                                            ci * OWN + (ti + 1) * 512],
                                    start=(ci == 0), stop=(ci == 7))
                        mmk = st.inc(mm, "pe")
                        if kkp == 7:
                            marks[("mmlast", ei)] = mmk
                        st.wait(s, "pe", mmk)
                        if ("hdma", ei - 2, hi) in hdma:
                            st.wait(s, "dsy", hdma[("hdma", ei - 2, hi)])
                        for tx in range(2):
                            ti = 2 * tp + tx
                            s.activation(h1_pb[par][:, hi * OWN + ti * 512:
                                                    hi * OWN + (ti + 1) * 512],
                                         ps_h[:, (bank + tx) * 512:(bank + tx + 1) * 512],
                                         AF.Gelu,
                                         bias=b1c[:, ei * NHI + hi:ei * NHI + hi + 1])
                            marks[("gelu", ei, hi * 4 + ti)] = st.inc(s.last, "act")
                        marks[("geluP", ei, kkp)] = marks[("gelu", ei, hi * 4 + 2 * tp + 1)]
                if ei < E:
                    st.wait(t, "dve", sel8_mk)
                    for ti in range(4):
                        gpar = ti % 4
                        if len(gb_free) >= 4:
                            st.wait(t, "dve", gb_free[-4])
                        mm = t.matmul(ps_gb[:, gpar * 512:(gpar + 1) * 512],
                                      sel8[:, ei * 128:(ei + 1) * 128],
                                      gtb_sb[0:8, ti * 512:(ti + 1) * 512],
                                      start=True, stop=True)
                        mkp = st.inc(mm, "pe")
                        st.wait(v, "pe", mkp)
                        for hi in range(NHI):
                            st.wait(v, "act", marks[("gelu", ei, hi * 4 + ti)])
                            v.tensor_mul(h1_pb[par][:, hi * OWN + ti * 512:
                                                    hi * OWN + (ti + 1) * 512],
                                         h1_pb[par][:, hi * OWN + ti * 512:
                                                    hi * OWN + (ti + 1) * 512],
                                         ps_gb[:, gpar * 512:(gpar + 1) * 512])
                        gb_free.append(st.inc(v.last, "dve"))
                    for hi in range(NHI):
                        st.wait(y, "dve", gb_free[-1])
                        hdma[("hdma", ei, hi)] = st.dma(
                            y, "dsy", h1g_d[ei, hi],
                            h1_pb[par][:, hi * OWN:(hi + 1) * OWN])
                else:
                    for hi in range(NHI):
                        st.wait(y, "act", marks[("gelu", ei, hi * 4 + 3)])
                        hdma[("hdma", ei, hi)] = st.dma(
                            y, "dsy", h1g_d[ei, hi],
                            h1_pb[par][:, hi * OWN:(hi + 1) * OWN])

            nc.all_engine_barrier()

        with ExitStack() as ph:
            pe_ = ph.enter_context
            h1g_pb = [pe_(nc.sbuf_tensor(f"h1g_pb{i}", [128, 36 * 128], BF16)) for i in range(3)]
            avf_pb = [pe_(nc.sbuf_tensor(f"avf_pb{i}", [128, 8 * 512], BF16)) for i in range(2)]
            x_pb = [pe_(nc.sbuf_tensor(f"x_pb{i}", [128, C], F32)) for i in range(3)]
            out_pb = [pe_(nc.sbuf_tensor(f"out_pb{i}", [128, 512], F32)) for i in range(4)]
            ps_o = pe_(nc.psum_tensor("ps_o", [128, 2048], F32))

            scr_done = st.cnt["dsy"]

            while w2_thunks:
                w2_thunks.pop(0)()
            w2_ld = st.cnt["din"]

            avmk, xmk, h1mk = {}, {}, {}

            def load_tgroup(tg):
                for j in range(NJ):
                    st.dma(y, "dsy", avf_pb[tg % 2][:, j * 512:(j + 1) * 512],
                           avt_d[j][:, tg * 512:(tg + 1) * 512])
                avmk[tg] = st.cnt["dsy"]

            def load_x(ts_):
                par = ts_ % 3
                st.dma(y, "dsy", x_pb[par][:, :], xo[ts_ * 128:(ts_ + 1) * 128, :])
                xmk[ts_] = st.cnt["dsy"]

            def load_h1(ts_):
                par = ts_ % 3
                for ei in range(NE):
                    for hi in range(NHI):
                        ch = ei * 4 + hi
                        dst = h1g_pb[par][:, ch * 128:(ch + 1) * 128]
                        src = h1g_d[ei, hi][:, ts_ * 128:(ts_ + 1) * 128]
                        if ch % 2 == 0:
                            st.dma(y, "dsy", dst, src)
                        else:
                            st.dma(s, "dsc", dst, src)
                h1mk[ts_] = (st.cnt["dsy"], st.cnt["dsc"])

            def load_tsub(ts_):
                load_x(ts_)
                load_h1(ts_)

            st.wait(y, "dsy", attn_scr_mk)
            load_tgroup(0)
            load_x(0)
            load_x(1)
            st.wait(y, "dsy", scr_done)
            st.wait(s, "dsy", scr_done)
            load_h1(0)
            load_h1(1)
            fr = {}
            for ts_ in range(16):
                tg = ts_ // 4
                par = ts_ % 3
                if ts_ + 2 < 16:
                    if ts_ >= 1:
                        y.wait_ge(SEM["pe"], fr[("mm", ts_ - 1)])
                        y.wait_ge(SEM["dve"], fr[("evac", (ts_ - 1) * 2 + 1)])
                        s.wait_ge(SEM["pe"], fr[("mm", ts_ - 1)])
                    if (ts_ + 2) % 4 == 0:
                        load_tgroup((ts_ + 2) // 4)
                    load_tsub(ts_ + 2)
                st.wait(t, "din", w2_ld)
                st.wait(t, "dsy", avmk[tg])
                opar = ts_ % 2
                ob = opar * 1024
                if ("mm", ts_ - 2) in fr:
                    st.wait(t, "dve", fr[("evac", (ts_ - 2) * 2 + 1)])
                for j in range(NJ):
                    lh = avf_pb[tg % 2][:, j * 512 + (ts_ % 4) * 128:
                                        j * 512 + (ts_ % 4 + 1) * 128]
                    t.matmul(ps_o[:, ob:ob + 512], lh,
                             wot_sb[:, j * 1024: j * 1024 + 512],
                             start=(j == 0), stop=False)
                    t.matmul(ps_o[:, ob + 512:ob + 1024], lh,
                             wot_sb[:, j * 1024 + 512: j * 1024 + 1024],
                             start=(j == 0), stop=False)
                st.wait(t, "dsy", h1mk[ts_][0])
                st.wait(t, "dsc", h1mk[ts_][1])
                for ei in range(NE):
                    for hi in range(NHI):
                        ch = ei * 4 + hi
                        lh = h1g_pb[par][:, ch * 128:(ch + 1) * 128]
                        t.matmul(ps_o[:, ob:ob + 512], lh,
                                 w2_sb[:, ch * 1024: ch * 1024 + 512],
                                 start=False, stop=False)
                        t.matmul(ps_o[:, ob + 512:ob + 1024], lh,
                                 w2_sb[:, ch * 1024 + 512: ch * 1024 + 1024],
                                 start=False, stop=False)
                lh = gtb_sb[:, ts_ * 128:(ts_ + 1) * 128]
                t.matmul(ps_o[:, ob:ob + 512], lh, b2_sb[:, 0:512],
                         start=False, stop=True)
                mm = t.matmul(ps_o[:, ob + 512:ob + 1024], lh, b2_sb[:, 512:1024],
                              start=False, stop=True)
                fr[("mm", ts_)] = st.inc(mm, "pe")
                st.wait(v, "pe", fr[("mm", ts_)])
                st.wait(v, "dsy", xmk[ts_])
                for cc in range(2):
                    oi = (ts_ * 2 + cc) % 4
                    if ("odma", ts_ * 2 + cc - 4) in fr:
                        st.wait(v, "dsy", fr[("odma", ts_ * 2 + cc - 4)])
                    v.tensor_add(out_pb[oi][:, :],
                                 ps_o[:, ob + cc * 512:ob + (cc + 1) * 512],
                                 x_pb[par][:, cc * 512:(cc + 1) * 512])
                    fr[("evac", ts_ * 2 + cc)] = st.inc(v.last, "dve")
                    st.wait(y, "dve", fr[("evac", ts_ * 2 + cc)])
                    fr[("odma", ts_ * 2 + cc)] = st.dma(
                        y, "dsy",
                        outd[ts_ * 128:(ts_ + 1) * 128, cc * 512:(cc + 1) * 512],
                        out_pb[oi][:, :])

            y.wait_ge(SEM["dsy"], st.cnt["dsy"])
            g.wait_ge(SEM["dsy"], st.cnt["dsy"])

    return nc


_NC_CACHE = {}


def kernel(**inputs):
    x = np.ascontiguousarray(np.asarray(inputs["L0_attn_x"], dtype=np.float32))
    if "nc" not in _NC_CACHE:
        _NC_CACHE["nc"] = build()
    nc = _NC_CACHE["nc"]

    com = {
        "wq": np.ascontiguousarray(inputs["L0_attn_Wq"], dtype=np.float32),
        "wk": np.ascontiguousarray(inputs["L0_attn_Wk"], dtype=np.float32),
        "wv": np.ascontiguousarray(inputs["L0_attn_Wv"], dtype=np.float32),
        "wo": np.ascontiguousarray(inputs["L0_attn_Wo"], dtype=np.float32),
        "gw": np.ascontiguousarray(inputs["L0_ffn_gate_W"], dtype=np.float32),
        "w1": np.ascontiguousarray(inputs["L0_ffn_W1"], dtype=np.float32),
        "b1": np.ascontiguousarray(inputs["L0_ffn_b1"], dtype=np.float32),
        "w2": np.ascontiguousarray(inputs["L0_ffn_W2"], dtype=np.float32),
        "b2": np.ascontiguousarray(inputs["L0_ffn_b2"], dtype=np.float32),
        "ws1": np.ascontiguousarray(inputs["L0_ffn_Ws1"], dtype=np.float32),
        "bs1": np.ascontiguousarray(
            np.asarray(inputs["L0_ffn_bs1"], dtype=np.float32).reshape(1, HID)),
        "ws2": np.ascontiguousarray(inputs["L0_ffn_Ws2"], dtype=np.float32),
        "bs2": np.ascontiguousarray(
            np.asarray(inputs["L0_ffn_bs2"], dtype=np.float32).reshape(1, C)),
        "rs": np.ascontiguousarray(
            np.asarray(inputs["L0_ffn_route_scale"], dtype=np.float32).reshape(1, 1)),
    }
    in_maps = []
    for c in range(8):
        b, half = c // 2, c % 2
        m = dict(com)
        m["xf"] = np.ascontiguousarray(x[b])
        m["xo"] = np.ascontiguousarray(x[b][own_rows(half)])
        in_maps.append(m)

    res = run_bass_kernel_spmd(nc, in_maps, core_ids=list(range(8)))
    out = np.empty((B, T, C), dtype=np.float32)
    for c in range(8):
        b, half = c // 2, c % 2
        out[b][own_rows(half)] = res.results[c]["out"]
    return out


if __name__ == "__main__":
    import importlib.util as _ilu
    spec = _ilu.spec_from_file_location("_ref", "/root/problem/reference.py")
    _r = _ilu.module_from_spec(spec)
    spec.loader.exec_module(_r)
    ins = {k: np.asarray(v) for k, v in _r.setup_inputs().items()}
    got = kernel(**ins)
    want = np.asarray(_r.reference(**_r.setup_inputs()))
    rv = float(np.mean((got - want) ** 2) / np.mean(want ** 2))
    print("max abs err:", np.abs(got - want).max(), "rvr:", rv)


# version 64 (final)
# speedup vs baseline: 1.0013x; 1.0013x over previous
import numpy as np
from contextlib import ExitStack

import concourse.bass as bass
import concourse.mybir as mybir
from concourse.bass_utils import run_bass_kernel_spmd

F32 = mybir.dt.float32
BF16 = mybir.dt.bfloat16
I32 = mybir.dt.int32
AF = mybir.ActivationFunctionType
ALU = mybir.AluOpType
AX = mybir.AxisListType

B, T, C = 4, 4096, 1024
DH, NJ = 64, 8
OWN = 2048
QTS, NQT = 512, 4
KCS = 128
E, NE, HID, NHI = 8, 9, 512, 4
EPS = 1e-5

Q0S = [[0, 512, 3072, 3584], [1024, 1536, 2048, 2560]]
LIMS = [[q0 // KCS + 4 for q0 in q0s] for q0s in Q0S]
OWN_SPANS = [[(0, 1024), (3072, 4096)], [(1024, 3072)]]


def own_rows(half):
    return np.concatenate([np.arange(a, b) for a, b in OWN_SPANS[half]])


def own_copy_spans(half):
    out, d = [], 0
    for a, b in OWN_SPANS[half]:
        out.append((d, a, b - a))
        d += b - a
    return out


def own_dma_spans(half):
    out = []
    for (d0, s0, ln) in own_copy_spans(half):
        for k in range(0, ln, 1024):
            out.append((d0 + k, s0 + k, 1024))
    return out


class Rec:

    def __init__(self, eng):
        object.__setattr__(self, "e", eng)
        object.__setattr__(self, "last", None)

    def __getattr__(self, k):
        a = getattr(self.e, k)
        if callable(a):
            def w(*args, **kw):
                r = a(*args, **kw)
                if hasattr(r, "then_inc"):
                    object.__setattr__(self, "last", r)
                return r
            return w
        return a


class St:
    def __init__(self, nc):
        self.nc = nc
        self.t, self.s, self.v, self.g, self.y = (
            Rec(nc.tensor), Rec(nc.scalar), Rec(nc.vector),
            Rec(nc.gpsimd), Rec(nc.sync))
        self.cnt = {}
        self.sems = {}

    def sem(self, stack, name):
        self.sems[name] = stack.enter_context(self.nc.semaphore(name))
        self.cnt[name] = 0

    def inc(self, inst, name, k=1):
        self.cnt[name] += k
        inst.then_inc(self.sems[name], k)
        return self.cnt[name]

    def wait(self, eng, name, val=None):
        if val is None:
            val = self.cnt[name]
        if val > 0:
            eng.wait_ge(self.sems[name], val)

    def dma(self, eng, sem, out, in_):
        return self.inc(eng.dma_start(out, in_), sem, 16)

    def vedge(self):
        mk = self.inc(self.v.last, "dve")
        self.wait(self.v, "dve", mk)
        return mk

    def snap(self):
        return dict(self.cnt)

    def restore(self, snapv):
        self.cnt = dict(snapv)


def build():
    nc = bass.Bass("TRN2", target_bir_lowering=False)
    st = St(nc)
    t, s, v, g, y = st.t, st.s, st.v, st.g, st.y

    xf = nc.dram_tensor("xf", [T, C], F32, kind="ExternalInput")
    xo = nc.dram_tensor("xo", [OWN, C], F32, kind="ExternalInput")
    wq = nc.dram_tensor("wq", [DH, DH], F32, kind="ExternalInput")
    wk = nc.dram_tensor("wk", [DH, DH], F32, kind="ExternalInput")
    wv = nc.dram_tensor("wv", [DH, DH], F32, kind="ExternalInput")
    wo = nc.dram_tensor("wo", [C, C], F32, kind="ExternalInput")
    gwd = nc.dram_tensor("gw", [C, E], F32, kind="ExternalInput")
    w1d = nc.dram_tensor("w1", [E, C, HID], F32, kind="ExternalInput")
    b1d = nc.dram_tensor("b1", [E, HID], F32, kind="ExternalInput")
    w2d = nc.dram_tensor("w2", [E, HID, C], F32, kind="ExternalInput")
    b2d = nc.dram_tensor("b2", [E, C], F32, kind="ExternalInput")
    ws1d = nc.dram_tensor("ws1", [C, HID], F32, kind="ExternalInput")
    bs1d = nc.dram_tensor("bs1", [1, HID], F32, kind="ExternalInput")
    ws2d = nc.dram_tensor("ws2", [HID, C], F32, kind="ExternalInput")
    bs2d = nc.dram_tensor("bs2", [1, C], F32, kind="ExternalInput")
    rsd = nc.dram_tensor("rs", [1, 1], F32, kind="ExternalInput")
    outd = nc.dram_tensor("out", [OWN, C], F32, kind="ExternalOutput")

    avt_d = nc.dram_tensor("avt_scr", [NJ, 128, OWN], BF16, kind="Internal")
    h1g_d = nc.dram_tensor("h1g_scr", [NE, NHI, 128, OWN], BF16, kind="Internal")
    xnt_d = nc.dram_tensor("xnt_scr", [128, 4 * T], BF16, kind="Internal")

    with ExitStack() as big:
        e_ = big.enter_context
        for name in ["din", "dsy", "dsc", "pe", "act", "dve", "pool"]:
            st.sem(big, name)
        SEM = st.sems

        idx = {}
        for name, eng in [("t", t), ("s", s), ("v", v), ("y", y), ("g", g)]:
            idx[name] = eng.compute_val(eng.partition_id() % 2)

        ident_b = e_(nc.sbuf_tensor("ident_b", [128, 128], BF16))
        ident_f = e_(nc.sbuf_tensor("ident_f", [128, 128], F32))
        mask_d = e_(nc.sbuf_tensor("mask_d", [128, 128], BF16))
        ones_b = e_(nc.sbuf_tensor("ones_b", [128, 1], BF16))
        ones_f = e_(nc.sbuf_tensor("ones_f", [1, 128], F32))
        ones_bh = e_(nc.sbuf_tensor("ones_bh", [1, 128], BF16))
        wqt = e_(nc.sbuf_tensor("wqt", [128, 128], BF16))
        wkt = e_(nc.sbuf_tensor("wkt", [128, 128], BF16))
        wvt = e_(nc.sbuf_tensor("wvt", [128, 128], BF16))
        wot_sb = e_(nc.sbuf_tensor("wot_sb", [128, NJ * 1024], BF16))
        gw_sb = e_(nc.sbuf_tensor("gw_sb", [128, NJ * E], BF16))
        b1c = e_(nc.sbuf_tensor("b1c", [128, NE * NHI], F32))
        rs_sb = e_(nc.sbuf_tensor("rs_sb", [1, 1], F32))
        rs_b = e_(nc.sbuf_tensor("rs_b", [128, 1], F32))
        negm = e_(nc.sbuf_tensor("negm", [128, 1], F32))
        m11 = e_(nc.sbuf_tensor("m11", [1, 4], F32))
        rstd_sb = e_(nc.sbuf_tensor("rstd_sb", [128, 32], F32))
        xnt_own = e_(nc.sbuf_tensor("xnt_own", [128, NJ * OWN], BF16))
        gtb_sb = e_(nc.sbuf_tensor("gtb_sb", [9, OWN], BF16))
        sel8 = e_(nc.sbuf_tensor("sel8", [8, E * 128], BF16))
        g0_sb = e_(nc.sbuf_tensor("g0_sb", [128, 16 * 8], F32))
        gt1 = e_(nc.sbuf_tensor("gt1", [128, 128], F32))
        gt2 = e_(nc.sbuf_tensor("gt2", [128, 128], F32))
        gsc = e_(nc.sbuf_tensor("gsc", [128, 5 * 16], F32))

        xg_stack = ExitStack()
        xnt_g = xg_stack.enter_context(nc.sbuf_tensor("xnt_g", [128, 4 * T], BF16))

        with ExitStack() as ph:
            pe_ = ph.enter_context
            xrow_b = pe_(nc.sbuf_tensor("xrow_b", [128, 4 * 4096], F32))
            xn_pb = [pe_(nc.sbuf_tensor(f"xn_pb{i}", [128, C], BF16)) for i in range(4)]
            xnt_hb = [pe_(nc.sbuf_tensor(f"xnt_hb{i}", [128, 2048], BF16)) for i in range(2)]
            stt_sb = pe_(nc.sbuf_tensor("stt_sb", [128, 2 * 48], F32))
            agg_sb = pe_(nc.sbuf_tensor("agg_sb", [128, 2 * 8], F32))
            iota_i = pe_(nc.sbuf_tensor("iota_i", [128, 512], I32))
            iota8 = pe_(nc.sbuf_tensor("iota8", [8, E * 128], I32))
            selt = pe_(nc.sbuf_tensor("selt", [8, E * 128], BF16))
            wtmp = pe_(nc.sbuf_tensor("wtmp", [128, 3 * DH], BF16))
            var_sb = pe_(nc.sbuf_tensor("var_sb", [128, 32], F32))
            ps_su = pe_(nc.psum_tensor("ps_su", [128, 512], BF16))
            ps_sf = pe_(nc.psum_tensor("ps_sf", [128, 512], F32))
            ps_x = pe_(nc.psum_tensor("ps_x", [128, 2048], F32))
            ps_xb = ps_x.bitcast(BF16)

            st.inc(g.iota(iota_i[:, :], [[1, 512]], channel_multiplier=-1), "pool")
            st.inc(g.iota(iota8[:, :], [[1, E * 128]], channel_multiplier=-128), "pool")
            iota_mk = st.cnt["pool"]
            st.dma(g, "din", wtmp[0:DH, 0:DH], wq[:, :])
            st.dma(g, "din", wtmp[0:DH, DH:2 * DH], wk[:, :])
            st.dma(g, "din", wtmp[0:DH, 2 * DH:3 * DH], wv[:, :])
            wqkv_ld = st.cnt["din"]
            st.wait(g, "din", wqkv_ld)
            for j in range(NJ):
                st.dma(g, "din", gw_sb[:, j * E:(j + 1) * E],
                       gwd[j * 128:(j + 1) * 128, :])
            for ei in range(E):
                for hi in range(NHI):
                    st.dma(g, "din", b1c[:, ei * NHI + hi:ei * NHI + hi + 1],
                           b1d[ei, hi * 128:(hi + 1) * 128].rearrange("(h o) -> h o", o=1))
            for hi in range(NHI):
                st.dma(g, "din", b1c[:, E * NHI + hi:E * NHI + hi + 1],
                       bs1d[0, hi * 128:(hi + 1) * 128].rearrange("(h o) -> h o", o=1))
            st.dma(g, "din", rs_sb[:, :], rsd[:, :])
            small_loads = st.cnt["din"]

            v.memset(ones_b[:, :], 1.0)
            v.memset(ones_f[:, :], 1.0)
            v.memset(ones_bh[:, :], 1.0)
            st.wait(v, "pool", iota_mk)
            v.tensor_single_scalar(ident_b[:, :], iota_i[:, 0:128], 0.0, ALU.is_equal)
            v.tensor_single_scalar(ident_f[:, :], iota_i[:, 0:128], 0.0, ALU.is_equal)
            v.tensor_single_scalar(mask_d[:, :], iota_i[:, 0:128], 0.0, ALU.is_ge)
            v.tensor_single_scalar(sel8[:, :], iota8[:, :], 0.0, ALU.is_ge)
            v.tensor_single_scalar(selt[:, :], iota8[:, :], 127.0, ALU.is_le)
            st.vedge()
            v.tensor_mul(sel8[:, :], sel8[:, :], selt[:, :])
            mk_ident = st.inc(v.last, "dve")
            sel8_mk = mk_ident

            st.wait(t, "din", wqkv_ld)
            st.wait(t, "dve", mk_ident)
            for i, wt_dst in enumerate([wqt, wkt, wvt]):
                v.memset(wt_dst[:, :], 0.0)
                st.vedge()
                t.transpose(ps_su[0:DH, 0:64], wtmp[0:DH, i * DH:(i + 1) * DH],
                            ident_b[0:DH, 0:DH])
                mk = st.inc(t.last, "pe")
                st.wait(v, "pe", mk)
                v.tensor_copy(wt_dst[0:DH, 0:DH], ps_su[0:DH, 0:64])
                v.tensor_copy(wt_dst[DH:128, DH:128], ps_su[0:DH, 0:64])
                mk2 = st.inc(v.last, "dve")
                st.wait(t, "dve", mk2)

            dma_mk, sw_sq, sw_math, sw_rstd = {}, {}, {}, {}
            sw_aff, sw_tr, sw_evs, sw_sp = {}, {}, {}, {}
            for it in range(11):
                if it < 8:
                    b = it
                    if b >= 4:
                        y.wait_ge(SEM["dve"], sw_aff[b - 4])
                    for k in range(4):
                        tt = 4 * b + k
                        qe, qs = (y, "dsy")
                        dma_mk[(b, k)] = (qs, st.dma(
                            qe, qs,
                            xrow_b[:, (b % 4) * 4096 + k * 1024:
                                   (b % 4) * 4096 + (k + 1) * 1024],
                            xf[tt * 128:(tt + 1) * 128, :]))
                    for k in range(4):
                        xs = xrow_b[:, (b % 4) * 4096 + k * 1024:
                                    (b % 4) * 4096 + (k + 1) * 1024]
                        st.wait(v, dma_mk[(b, k)][0], dma_mk[(b, k)][1])
                        for hf in range(2):
                            v.bn_stats(
                                stt_sb[:, (b % 2) * 48 + k * 12 + hf * 6:
                                       (b % 2) * 48 + k * 12 + (hf + 1) * 6].rearrange(
                                    "p (j s) -> p j s", s=6),
                                xs[:, hf * 512:(hf + 1) * 512].rearrange(
                                    "p (j c) -> p j c", j=1))
                        sw_sq[b] = st.inc(v.last, "dve")
                if 1 <= it <= 8:
                    b = it - 1
                    c4 = slice(b * 4, (b + 1) * 4)
                    st.wait(v, "dve", sw_sq[b])
                    for k in range(4):
                        v.bn_aggr(
                            agg_sb[:, (b % 2) * 8 + k * 2:(b % 2) * 8 + k * 2 + 2],
                            stt_sb[:, (b % 2) * 48 + k * 12:(b % 2) * 48 + (k + 1) * 12])
                    st.vedge()
                    v.tensor_scalar(
                        var_sb[:, c4],
                        agg_sb[:, (b % 2) * 8:(b % 2) * 8 + 8].rearrange(
                            "p (j s) -> p j s", s=2)[:, :, 1:2].rearrange(
                            "p j o -> p (j o)"),
                        EPS, None, ALU.add)
                    sw_math[b] = st.inc(v.last, "dve")
                    st.wait(s, "dve", sw_math[b])
                    s.activation(var_sb[:, c4], var_sb[:, c4], AF.Ln)
                    mk = st.inc(s.last, "act")
                    st.wait(s, "act", mk)
                    s.activation(rstd_sb[:, c4], var_sb[:, c4],
                                 AF.Exp, scale=-0.5)
                    sw_rstd[b] = st.inc(s.last, "act")
                if 2 <= it <= 9:
                    b = it - 2
                    for k in range(4):
                        tt = 4 * b + k
                        slot = k * 1024
                        xs = xrow_b[:, (b % 4) * 4096 + k * 1024:
                                    (b % 4) * 4096 + (k + 1) * 1024]
                        st.wait(v, "act", sw_rstd[b])
                        if tt >= 4:
                            st.wait(v, "pe", sw_tr[tt - 4])
                        v.tensor_scalar(xn_pb[tt % 4][:, :], xs,
                                        agg_sb[:, (b % 2) * 8 + k * 2:
                                               (b % 2) * 8 + k * 2 + 1],
                                        rstd_sb[:, tt:tt + 1],
                                        ALU.subtract, ALU.mult)
                        sw_aff[b] = st.inc(v.last, "dve")
                        st.wait(t, "dve", sw_aff[b])
                        if b >= 1:
                            st.wait(t, "act", sw_evs[(b - 1, k)])
                        for ci in range(8):
                            t.transpose(
                                ps_xb[:, slot + ci * 128: slot + (ci + 1) * 128],
                                xn_pb[tt % 4][:, ci * 128:(ci + 1) * 128],
                                ident_b[:, :])
                        sw_tr[tt] = st.inc(t.last, "pe")
                        st.wait(s, "pe", sw_tr[tt])
                        s.copy(
                            xnt_g[:, :].rearrange("p (c n) -> p c n", c=4)[:, :, tt * 128:(tt + 1) * 128],
                            ps_xb[:, slot:slot + 512].rearrange("p (c n) -> p c n", c=4))
                        if b >= 2 and k == 0:
                            st.wait(s, "din", sw_sp[b - 2])
                        s.copy(
                            xnt_hb[b % 2][:, :].rearrange("p (c n) -> p c n", c=4)[:, :, k * 128:(k + 1) * 128],
                            ps_xb[:, slot + 512:slot + 1024].rearrange("p (c n) -> p c n", c=4))
                        sw_evs[(b, k)] = st.inc(s.last, "act")
                        if k == 3:
                            st.wait(g, "act", sw_evs[(b, 3)])
                            sw_sp[b] = st.dma(
                                g, "din",
                                xnt_d[:, :].rearrange("p (c n) -> p c n", c=4)[:, :, b * 512:(b + 1) * 512],
                                xnt_hb[b % 2][:, :].rearrange("p (c n) -> p c n", c=4))

            st.wait(v, "act", sw_evs[(7, 3)])
            st.wait(y, "din", sw_sp[7])
            snapv = st.snap()
            ends = []
            for bi in nc.Switch(engines=[v.e, y.e], index=[idx["v"], idx["y"]], n=2):
                st.restore(snapv)
                for jl in range(4):
                    for (d0, s0, ln) in own_copy_spans(bi):
                        v.tensor_copy(
                            xnt_own[:, jl * OWN + d0: jl * OWN + d0 + ln],
                            xnt_g[:, jl * T + s0: jl * T + s0 + ln])
                st.inc(v.last, "dve")
                for jl in range(4, 8):
                    for (d0, s0, ln) in own_dma_spans(bi):
                        st.dma(y, "dsy", xnt_own[:, jl * OWN + d0: jl * OWN + d0 + ln],
                               xnt_d[:, (jl - 4) * T + s0:(jl - 4) * T + s0 + ln])
                ends.append(st.snap())
            assert ends[0] == ends[1]
            st.restore(ends[0])
            own_gather_mk = st.cnt["dsy"]

            st.wait(t, "din", small_loads)
            t.matmul(ps_sf[:, 0:1], ones_f[0:1, :], rs_sb[0:1, 0:1],
                     start=True, stop=True)
            mk = st.inc(t.last, "pe")
            st.wait(v, "pe", mk)
            v.tensor_copy(rs_b[:, :], ps_sf[:, 0:1])
            st.inc(v.last, "dve")


            nc.all_engine_barrier()

        nc.all_engine_barrier()

        for grp in range(2):
            with ExitStack() as ph:
                pe_ = ph.enter_context
                kt_sb = pe_(nc.sbuf_tensor(f"kt_sb{grp}", [128, 4 * T], BF16))
                v_sb = pe_(nc.sbuf_tensor(f"v_sb{grp}", [128, 32 * 8 * 65], BF16))
                qt_sb = pe_(nc.sbuf_tensor(f"qt_sb{grp}", [128, 4 * OWN], BF16))
                ps_sc = pe_(nc.psum_tensor(f"ps_sc{grp}", [128, 2048], F32))
                ps_av_s = [pe_(nc.psum_tensor(f"ps_av{grp}_{i}", [128, 1024], F32))
                           for i in range(2)]
                pha = ph.enter_context(ExitStack())
                sqm = pha.enter_context(
                    nc.sbuf_tensor(f"sqm{grp}", [128, 6 * 2048], BF16))

                st.wait(t, "dve", st.cnt["dve"])
                if grp == 1:
                    st.wait(t, "din", xnt_reload_mk)
                    st.wait(y, "din", wo_ld)
                    for a in range(8):
                        for b in range(8):
                            st.inc(y.dma_start_transpose(
                                wot_sb[:, b * 1024 + a * 128: b * 1024 + (a + 1) * 128],
                                xnt_d[:, a * 1024 + b * 128: a * 1024 + (b + 1) * 128]),
                                "dsy", 16)
                kq_work = []
                for jl in range(4):
                    for ti in range(8):
                        kq_work.append(("k", jl, ti))
                for jl in range(4):
                    for ti in range(4):
                        kq_work.append(("q", jl, ti))
                mm_mks, ev_mks, ev_eng = [], [], []

                def kq_mm(i):
                    kind, jl, ti = kq_work[i]
                    sl = i % 4
                    if grp == 0 and i == 32:
                        st.wait(t, "dsy", own_gather_mk)
                    if i >= 4:
                        st.wait(t, "dve" if ev_eng[i - 4] == "v" else "act",
                                ev_mks[i - 4])
                    if kind == "k":
                        mm = t.matmul(ps_sc[:, sl * 512:(sl + 1) * 512],
                                      wkt[:, :],
                                      xnt_g[:, jl * T + ti * 512:
                                            jl * T + (ti + 1) * 512],
                                      start=True, stop=True)
                    else:
                        jglob = grp * 4 + jl
                        mm = t.matmul(ps_sc[:, sl * 512:(sl + 1) * 512],
                                      wqt[:, :],
                                      xnt_own[:, jglob * OWN + ti * 512:
                                              jglob * OWN + (ti + 1) * 512],
                                      start=True, stop=True)
                    mm_mks.append(st.inc(mm, "pe"))

                def kq_ev(i):
                    kind, jl, ti = kq_work[i]
                    sl = i % 4
                    dst = (kt_sb[:, jl * T + ti * 512: jl * T + (ti + 1) * 512]
                           if kind == "k" else
                           qt_sb[:, jl * OWN + ti * 512: jl * OWN + (ti + 1) * 512])
                    if i % 2 == 0:
                        st.wait(v, "pe", mm_mks[i])
                        v.tensor_copy(dst, ps_sc[:, sl * 512:(sl + 1) * 512])
                        ev_mks.append(st.inc(v.last, "dve"))
                        ev_eng.append("v")
                    else:
                        st.wait(s, "pe", mm_mks[i])
                        s.copy(dst, ps_sc[:, sl * 512:(sl + 1) * 512])
                        ev_mks.append(st.inc(s.last, "act"))
                        ev_eng.append("s")

                for i in range(len(kq_work)):
                    kq_mm(i)
                    if i >= 1:
                        kq_ev(i - 1)
                kq_ev(len(kq_work) - 1)

                waves = ([("q", jl) for jl in range(4)]
                         + [("k0", jl) for jl in range(4)]
                         + [("k2", jl) for jl in range(4)])

                def wave_src(w):
                    kind, jl = waves[w]
                    if kind == "q":
                        return qt_sb[:, jl * OWN: jl * OWN + 2048]
                    if kind == "k0":
                        return kt_sb[:, jl * T: jl * T + 2048]
                    return kt_sb[:, jl * T + 2048: jl * T + 4096]

                sq_mk, nm_mk = [], []

                def emit_square(w):
                    sl = (w % 6) * 2048
                    s.activation(sqm[:, sl:sl + 2048], wave_src(w), AF.Square)
                    sq_mk.append(st.inc(s.last, "act"))

                st.wait(s, "dve", st.cnt["dve"])
                for w in range(6):
                    emit_square(w)

                v.memset(v_sb[:, :].rearrange("p (a b) -> p a b", b=65)[:, :, 64:65], 1.0)
                st.inc(v.last, "dve")
                st.wait(t, "dve", st.cnt["dve"])
                vps = ps_av_s
                vmm_mks, vev_mks, vev_eng = [], [], []

                def v_mm(i):
                    jl, tt = divmod(i, 32)
                    ps = vps[i % 2]
                    if i >= 2:
                        st.wait(t, "dve" if vev_eng[i - 2] == "v" else "act",
                                vev_mks[i - 2])
                    mm = t.matmul(
                        ps[:, 0:128],
                        xnt_g[:, jl * T + tt * 128: jl * T + (tt + 1) * 128],
                        wvt[:, :],
                        start=True, stop=True)
                    vmm_mks.append(st.inc(mm, "pe"))

                def v_ev(i):
                    jl, tt = divmod(i, 32)
                    ps = vps[i % 2]
                    st.wait(v, "pe", vmm_mks[i])
                    v.tensor_copy(
                        v_sb[:, tt * 520 + (jl * 2) * 65:
                             tt * 520 + (jl * 2) * 65 + 130].rearrange(
                            "p (h n) -> p h n", n=65)[:, :, 0:64],
                        ps[:, 0:128].rearrange("p (h n) -> p h n", n=64))
                    vev_mks.append(st.inc(v.last, "dve"))
                    vev_eng.append("v")

                for i in range(128):
                    v_mm(i)
                    if i >= 1:
                        v_ev(i - 1)
                v_ev(127)

                st.wait(t, "dve", st.cnt["dve"])
                st.wait(t, "act", st.cnt["act"])
                for w in range(12):
                    sl = (w % 6) * 2048
                    jl = w % 4
                    st.wait(t, "act", sq_mk[w])
                    for ti in range(4):
                        mm = t.matmul(ps_sc[0:1, ti * 512:(ti + 1) * 512],
                                      ones_b[:, 0:1],
                                      sqm[:, sl + ti * 512: sl + (ti + 1) * 512],
                                      start=(jl == 0), stop=(jl == 3))
                    nm_mk.append(st.inc(mm, "pe"))
                    if w + 6 < 12:
                        st.wait(s, "pe", nm_mk[w])
                        emit_square(w + 6)
                    if jl == 3:
                        st.wait(v, "pe", nm_mk[w])
                        v.reduce_max(m11[0:1, w // 4: w // 4 + 1],
                                     ps_sc[0:1, 0:2048], AX.X)
                        mk = st.inc(v.last, "dve")
                        st.wait(t, "dve", mk)
                v.tensor_max(m11[0:1, 1:2], m11[0:1, 1:2], m11[0:1, 2:3])
                st.vedge()
                v.tensor_mul(m11[0:1, 0:1], m11[0:1, 0:1], m11[0:1, 1:2])
                mk = st.inc(v.last, "dve")
                st.wait(s, "dve", mk)
                s.activation(m11[0:1, 3:4], m11[0:1, 0:1], AF.Sqrt)
                mk = st.inc(s.last, "act")
                st.wait(v, "act", mk)
                v.tensor_scalar(m11[0:1, 3:4], m11[0:1, 3:4], -0.1275, None, ALU.mult)
                mk = st.inc(v.last, "dve")
                st.wait(t, "dve", mk)
                t.matmul(ps_av_s[0][:, 0:1], ones_f[0:1, :], m11[0:1, 3:4],
                         start=True, stop=True)
                mk = st.inc(t.last, "pe")
                st.wait(v, "pe", mk)
                v.tensor_copy(negm[:, :], ps_av_s[0][:, 0:1])
                st.inc(v.last, "dve")

                nc.all_engine_barrier()
                if grp == 0:
                    xnt_reload_mk = st.dma(g, "din", xnt_g[:, :], xnt_d[:, :])
                    st.wait(g, "din", xnt_reload_mk)
                    for ic in range(8):
                        wo_ld = st.dma(
                            g, "din", xnt_d[:, ic * 1024:(ic + 1) * 1024],
                            wo[ic * 128:(ic + 1) * 128, :])

                pha.close()
                phb = ph.enter_context(ExitStack())
                pb_ = phb.enter_context
                pt_pb = [pb_(nc.sbuf_tensor(f"pt{grp}_{i}", [128, 1024], BF16)) for i in range(4)]
                av_pb = [pb_(nc.sbuf_tensor(f"av{grp}_{i}", [128, 512], BF16)) for i in range(4)]
                rz_sb = pb_(nc.sbuf_tensor(f"rz{grp}", [1, 1024], BF16))
                rzl_sb = pb_(nc.sbuf_tensor(f"rzl{grp}", [1, 1024], F32))
                rzb_sb = pb_(nc.sbuf_tensor(f"rzb{grp}", [64, 1024], F32))

                snapv = st.snap()
                ends = []
                for bi in nc.Switch(engines=[t.e, s.e, v.e, y.e],
                                    index=[idx["t"], idx["s"], idx["v"], idx["y"]],
                                    n=2):
                    st.restore(snapv)
                    q0s, lims = Q0S[bi], LIMS[bi]
                    slot_free = {0: 0, 1: 0}
                    prev_av4 = [0, 0, 0, 0]
                    prev_exmk = [0, 0]
                    dma_mks = {}
                    pending = [None]
                    pq_list = [(pl, qt) for pl in range(4) for qt in range(NQT)]

                    def pq_ctx(pqi):
                        pl, qt = pq_list[pqi]
                        q0, L = q0s[qt], lims[qt]
                        order = list(range(L - 4, L)) + list(range(0, L - 4))
                        return pl, qt, q0, L, order

                    def emit_scores(pqi, j, exwait):
                        pl, qt, q0, L, order = pq_ctx(pqi)
                        kc = pq_ctx(pqi)[4][j]
                        scp = j % 2
                        coff = kc * 128 - q0
                        sk = coff if coff >= 128 else 0
                        st.wait(t, "act", exwait)
                        for h in range(2):
                            mm = t.matmul(
                                ps_sc[:, scp * 1024 + h * 512 + sk:
                                      scp * 1024 + (h + 1) * 512],
                                kt_sb[h * 64:(h + 1) * 64,
                                      pl * T + kc * 128: pl * T + (kc + 1) * 128],
                                qt_sb[h * 64:(h + 1) * 64,
                                      pl * OWN + qt * 512 + sk:
                                      pl * OWN + (qt + 1) * 512],
                                start=True, stop=True,
                                tile_position=(64 * h, 0))
                        return st.inc(mm, "pe")

                    def run_pq(pqi, sc_pre):
                        pl, qt, q0, L, order = pq_ctx(pqi)
                        jglob = grp * 4 + pl
                        av_mk, ex_mk, sc_mk, msk_mk = [], [], [], {}
                        if sc_pre is not None:
                            sc_mk.append(sc_pre)

                        def emit_exp_mask(j):
                            kc = order[j]
                            p3 = j % 4
                            scp = j % 2
                            st.wait(s, "pe", sc_mk[j])
                            st.wait(s, "pe",
                                    av_mk[j - 4] if j >= 4 else prev_av4[j])
                            coff = kc * 128 - q0
                            if coff >= 128:
                                for h in range(2):
                                    ai = s.activation(
                                        pt_pb[p3][:, h * 512 + coff:(h + 1) * 512],
                                        ps_sc[:, scp * 1024 + h * 512 + coff:
                                              scp * 1024 + (h + 1) * 512],
                                        AF.Exp, bias=negm[:, 0:1], scale=0.125)
                                ex_mk.append(st.inc(ai, "act"))
                            else:
                                ai = s.activation(pt_pb[p3][:, :],
                                                  ps_sc[:, scp * 1024: scp * 1024 + 1024],
                                                  AF.Exp, bias=negm[:, 0:1], scale=0.125)
                                ex_mk.append(st.inc(ai, "act"))
                            if coff >= 0:
                                st.wait(v, "act", ex_mk[j])
                                for h in range(2):
                                    off = h * 512 + coff
                                    v.tensor_mul(pt_pb[p3][:, off:off + 128],
                                                 pt_pb[p3][:, off:off + 128],
                                                 mask_d[:, :])
                                msk_mk[j] = st.inc(v.last, "dve")

                        def emit_av(j):
                            kc = order[j]
                            p3 = j % 4
                            coff = max(0, kc * 128 - q0)
                            if j in msk_mk:
                                st.wait(t, "dve", msk_mk[j])
                            else:
                                st.wait(t, "act", ex_mk[j])
                            if j == 0 and slot_free[pqi % 2]:
                                st.wait(t, "dve", slot_free[pqi % 2])
                            for h in range(2):
                                hloc = pl * 2 + h
                                mm = t.matmul(
                                    ps_av_s[pqi % 2][0:65, h * 512 + coff:(h + 1) * 512],
                                    v_sb[:, kc * 520 + hloc * 65:
                                         kc * 520 + hloc * 65 + 65],
                                    pt_pb[p3][:, h * 512 + coff:(h + 1) * 512],
                                    start=(j == 0), stop=(j == L - 1))
                            av_mk.append(st.inc(mm, "pe"))

                        for j in range(L):
                            if not (j == 0 and sc_pre is not None):
                                sc_mk.append(emit_scores(
                                    pqi, j,
                                    ex_mk[j - 2] if j >= 2 else prev_exmk[j % 2]))
                            if j >= 1:
                                emit_exp_mask(j - 1)
                            if j == 1 and pending[0] is not None:
                                pending[0][0]()
                            if j == 2 and pending[0] is not None:
                                pending[0][1]()
                                pending[0] = None
                            if j >= 2:
                                emit_av(j - 2)
                        emit_exp_mask(L - 1)
                        nxt = None
                        if pqi + 1 < len(pq_list):
                            nxt = emit_scores(pqi + 1, 0,
                                              ex_mk[L - 2] if L >= 2 else 0)
                        emit_av(L - 2)
                        emit_av(L - 1)
                        for _i4 in range(4):
                            prev_av4[_i4] = av_mk[L - 4 + ((_i4 - (L - 4)) % 4)]
                        prev_exmk[0], prev_exmk[1] = (
                            ex_mk[L - 2] if (L - 2) % 2 == 0 else ex_mk[L - 1],
                            ex_mk[L - 1] if (L - 1) % 2 == 1 else ex_mk[L - 2])

                        def mk_pending(pq2=pqi, jglob=jglob, qt=qt,
                                       last_av=av_mk[-1]):
                            state = {}
                            pav = ps_av_s[pq2 % 2]

                            def run_s():
                                st.wait(s, "pe", last_av)
                                s.activation(rzl_sb[0:1, :], pav[64:65, 0:1024],
                                             AF.Ln)
                                mkl = st.inc(s.last, "act")
                                st.wait(s, "act", mkl)
                                s.activation(rz_sb[0:1, :], rzl_sb[0:1, :],
                                             AF.Exp, scale=-1.0)
                                state["rz"] = st.inc(s.last, "act")

                            def run_tv():
                                par2 = pq2 % 4
                                st.wait(t, "act", state["rz"])
                                for h in range(2):
                                    mm = t.matmul(pav[64:128, h * 512:(h + 1) * 512],
                                                  ones_bh[0:1, 0:64],
                                                  rz_sb[0:1, h * 512:(h + 1) * 512],
                                                  start=True, stop=True,
                                                  tile_position=(0, 64))
                                bc_mk = st.inc(mm, "pe")
                                st.wait(v, "pe", bc_mk)
                                if pq2 >= 4:
                                    st.wait(v, "dsy", dma_mks[pq2 - 4])
                                v.tensor_copy(rzb_sb[:, :], pav[64:128, 0:1024])
                                st.vedge()
                                for h in range(2):
                                    v.scalar_tensor_tensor(
                                        av_pb[par2][h * 64:(h + 1) * 64, :],
                                        pav[0:64, h * 512:(h + 1) * 512], 1.0,
                                        rzb_sb[0:64, h * 512:(h + 1) * 512],
                                        ALU.mult, ALU.mult)
                                slot_free[pq2 % 2] = st.inc(v.last, "dve")
                                st.wait(y, "dve", slot_free[pq2 % 2])
                                dma_mks[pq2] = st.dma(
                                    y, "dsy",
                                    avt_d[jglob, :, qt * 512:(qt + 1) * 512],
                                    av_pb[par2][:, :])
                            return (run_s, run_tv)

                        pending[0] = mk_pending()
                        return nxt

                    sc_next = None
                    for pqi in range(len(pq_list)):
                        sc_next = run_pq(pqi, sc_next)
                    pending[0][0]()
                    pending[0][1]()
                    ends.append(st.snap())
                assert ends[0] == ends[1], (ends[0], ends[1])
                st.restore(ends[0])

                nc.all_engine_barrier()

        attn_scr_mk = st.cnt["dsy"]
        xg_stack.close()

        with ExitStack() as ph:
            pe_ = ph.enter_context
            ps_g = pe_(nc.psum_tensor("ps_g", [128, 128], F32))
            ps_gt = pe_(nc.psum_tensor("ps_gt", [128, 2048], F32))
            st.wait(t, "din", small_loads)
            for ts_ in range(16):
                for j in range(NJ):
                    mm = t.matmul(ps_g[:, ts_ * 8:(ts_ + 1) * 8],
                                  xnt_own[:, j * OWN + ts_ * 128: j * OWN + (ts_ + 1) * 128],
                                  gw_sb[:, j * E:(j + 1) * E],
                                  start=(j == 0), stop=(j == NJ - 1))
            mk = st.inc(mm, "pe")
            st.wait(s, "pe", mk)
            s.activation(g0_sb[:, :], ps_g[:, :], AF.Sigmoid)
            gmk_sig = st.inc(s.last, "act")
            st.wait(v, "act", gmk_sig)
            g3 = g0_sb[:, :].rearrange("p (t e) -> p t e", e=E)
            gt1_3 = gt1[:, :].rearrange("p (t e) -> p t e", e=E)
            gt2_3 = gt2[:, :].rearrange("p (t e) -> p t e", e=E)

            def col(i):
                return gsc[:, i * 16:(i + 1) * 16]

            def colb(i):
                return gsc[:, i * 16:(i + 1) * 16].rearrange(
                    "p (t o) -> p t o", o=1).broadcast_to([128, 16, E])

            def col3(i):
                return gsc[:, i * 16:(i + 1) * 16].rearrange("p (t o) -> p t o", o=1)

            v.tensor_scalar(g0_sb[:, :], g0_sb[:, :], rs_b[:, 0:1], None, ALU.mult)
            st.vedge()
            v.reduce_max(col3(0), g3, AX.X)
            st.vedge()
            v.tensor_tensor(gt1_3, g3, colb(0), ALU.is_ge)
            st.vedge()
            v.scalar_tensor_tensor(gt2[:, :], gt1[:, :], -1e30, g0_sb[:, :],
                                   ALU.mult, ALU.add)
            st.vedge()
            v.reduce_max(col3(1), gt2_3, AX.X)
            st.vedge()
            v.tensor_tensor(gt2_3, gt2_3, colb(1), ALU.is_ge)
            st.vedge()
            v.tensor_add(col(2), col(0), col(1))
            st.vedge()
            v.reciprocal(col(2), col(2))
            st.vedge()
            v.tensor_mul(col(3), col(0), col(2))
            v.tensor_mul(col(4), col(1), col(2))
            st.vedge()
            v.tensor_tensor(gt1_3, gt1_3, colb(3), ALU.mult)
            v.tensor_tensor(gt2_3, gt2_3, colb(4), ALU.mult)
            st.vedge()
            v.tensor_add(g0_sb[:, :], gt1[:, :], gt2[:, :])
            gv_all = st.inc(v.last, "dve")
            st.wait(t, "dve", gv_all)
            for ts_ in range(16):
                t.transpose(ps_gt[0:8, ts_ * 128:(ts_ + 1) * 128],
                            g0_sb[:, ts_ * 8:(ts_ + 1) * 8], ident_f[:, :])
            mk = st.inc(t.last, "pe")
            st.wait(v, "pe", mk)
            v.memset(gtb_sb[0:9, :], 1.0)
            st.vedge()
            v.tensor_copy(gtb_sb[0:8, :], ps_gt[0:8, 0:2048])
            st.inc(v.last, "dve")

            nc.all_engine_barrier()

        ffn_outer = big.enter_context(ExitStack())
        fo_ = ffn_outer.enter_context
        w2_sb = fo_(nc.sbuf_tensor("w2_sb", [128, 40 * 1024], BF16))
        b2_sb = fo_(nc.sbuf_tensor("b2_sb", [E + 1, C], BF16))
        w2_thunks = []
        w2_thunks.append(lambda: st.dma(g, "din", b2_sb[0:E, :], b2d[:, :]))
        w2_thunks.append(lambda: st.dma(g, "din", b2_sb[E:E + 1, :], bs2d[:, :]))
        for _ei in range(NE):
            _src = w2d[_ei] if _ei < E else ws2d
            for _hi in range(NHI):
                def _th(src=_src, hi=_hi, ch=_ei * 4 + _hi):
                    st.dma(g, "din", w2_sb[:, ch * 1024:(ch + 1) * 1024],
                           src[hi * 128:(hi + 1) * 128, :])
                w2_thunks.append(_th)

        with ExitStack() as ph:
            pe_ = ph.enter_context
            w1_pb = [pe_(nc.sbuf_tensor(f"w1_pb{i}", [128, 8 * 512], BF16)) for i in range(2)]
            h1_pb = [pe_(nc.sbuf_tensor(f"h1_pb{i}", [128, NHI * OWN], BF16)) for i in range(2)]
            ps_h = pe_(nc.psum_tensor("ps_h", [128, 2048], F32))
            ps_gb = pe_(nc.psum_tensor("ps_gb", [128, 2048], F32))

            w1mk = []

            def load_w1(ei):
                par = ei % 2
                src = w1d[ei] if ei < E else ws1d
                for ci in range(8):
                    st.dma(g, "din", w1_pb[par][:, ci * 512:(ci + 1) * 512],
                           src[ci * 128:(ci + 1) * 128, :])
                w1mk.append(st.cnt["din"])
                for _ in range(5):
                    if w2_thunks:
                        w2_thunks.pop(0)()

            load_w1(0)
            hdma, marks = {}, {}
            gb_free = []
            for ei in range(NE):
                par = ei % 2
                if ei + 1 < NE:
                    if ei >= 1:
                        g.wait_ge(SEM["pe"], marks[("mmlast", ei - 1)])
                    load_w1(ei + 1)
                st.wait(t, "din", w1mk[ei])
                for hi in range(NHI):
                    for tp in range(2):
                        kkp = hi * 2 + tp
                        bank = (kkp % 2) * 2
                        key = (("geluP", ei, kkp - 2) if kkp >= 2
                               else ("geluP", ei - 1, kkp + 6))
                        if key in marks:
                            st.wait(t, "act", marks[key])
                        for ci in range(8):
                            for tx in range(2):
                                ti = 2 * tp + tx
                                mm = t.matmul(
                                    ps_h[:, (bank + tx) * 512:(bank + tx + 1) * 512],
                                    w1_pb[par][:, ci * 512 + hi * 128:
                                               ci * 512 + (hi + 1) * 128],
                                    xnt_own[:, ci * OWN + ti * 512:
                                            ci * OWN + (ti + 1) * 512],
                                    start=(ci == 0), stop=(ci == 7))
                        mmk = st.inc(mm, "pe")
                        if kkp == 7:
                            marks[("mmlast", ei)] = mmk
                        st.wait(s, "pe", mmk)
                        if ("hdma", ei - 2, hi) in hdma:
                            st.wait(s, "dsy", hdma[("hdma", ei - 2, hi)])
                        for tx in range(2):
                            ti = 2 * tp + tx
                            s.activation(h1_pb[par][:, hi * OWN + ti * 512:
                                                    hi * OWN + (ti + 1) * 512],
                                         ps_h[:, (bank + tx) * 512:(bank + tx + 1) * 512],
                                         AF.Gelu,
                                         bias=b1c[:, ei * NHI + hi:ei * NHI + hi + 1])
                            marks[("gelu", ei, hi * 4 + ti)] = st.inc(s.last, "act")
                        marks[("geluP", ei, kkp)] = marks[("gelu", ei, hi * 4 + 2 * tp + 1)]
                if ei < E:
                    st.wait(t, "dve", sel8_mk)
                    for ti in range(4):
                        gpar = ti % 4
                        if len(gb_free) >= 4:
                            st.wait(t, "dve", gb_free[-4])
                        mm = t.matmul(ps_gb[:, gpar * 512:(gpar + 1) * 512],
                                      sel8[:, ei * 128:(ei + 1) * 128],
                                      gtb_sb[0:8, ti * 512:(ti + 1) * 512],
                                      start=True, stop=True)
                        mkp = st.inc(mm, "pe")
                        st.wait(v, "pe", mkp)
                        for hi in range(NHI):
                            st.wait(v, "act", marks[("gelu", ei, hi * 4 + ti)])
                            v.tensor_mul(h1_pb[par][:, hi * OWN + ti * 512:
                                                    hi * OWN + (ti + 1) * 512],
                                         h1_pb[par][:, hi * OWN + ti * 512:
                                                    hi * OWN + (ti + 1) * 512],
                                         ps_gb[:, gpar * 512:(gpar + 1) * 512])
                        gb_free.append(st.inc(v.last, "dve"))
                    for hi in range(NHI):
                        st.wait(y, "dve", gb_free[-1])
                        hdma[("hdma", ei, hi)] = st.dma(
                            y, "dsy", h1g_d[ei, hi],
                            h1_pb[par][:, hi * OWN:(hi + 1) * OWN])
                else:
                    for hi in range(NHI):
                        st.wait(y, "act", marks[("gelu", ei, hi * 4 + 3)])
                        hdma[("hdma", ei, hi)] = st.dma(
                            y, "dsy", h1g_d[ei, hi],
                            h1_pb[par][:, hi * OWN:(hi + 1) * OWN])

            nc.all_engine_barrier()

        with ExitStack() as ph:
            pe_ = ph.enter_context
            h1g_pb = [pe_(nc.sbuf_tensor(f"h1g_pb{i}", [128, 36 * 128], BF16)) for i in range(3)]
            avf_pb = [pe_(nc.sbuf_tensor(f"avf_pb{i}", [128, 8 * 512], BF16)) for i in range(2)]
            x_pb = [pe_(nc.sbuf_tensor(f"x_pb{i}", [128, C], F32)) for i in range(3)]
            out_pb = [pe_(nc.sbuf_tensor(f"out_pb{i}", [128, 512], F32)) for i in range(4)]
            ps_o = pe_(nc.psum_tensor("ps_o", [128, 2048], F32))

            scr_done = st.cnt["dsy"]

            while w2_thunks:
                w2_thunks.pop(0)()
            w2_ld = st.cnt["din"]

            avmk, xmk, h1mk = {}, {}, {}

            def load_tgroup(tg):
                for j in range(NJ):
                    st.dma(y, "dsy", avf_pb[tg % 2][:, j * 512:(j + 1) * 512],
                           avt_d[j][:, tg * 512:(tg + 1) * 512])
                avmk[tg] = st.cnt["dsy"]

            def load_x(ts_):
                par = ts_ % 3
                st.dma(y, "dsy", x_pb[par][:, :], xo[ts_ * 128:(ts_ + 1) * 128, :])
                xmk[ts_] = st.cnt["dsy"]

            def load_h1(ts_):
                par = ts_ % 3
                for ei in range(NE):
                    for hi in range(NHI):
                        ch = ei * 4 + hi
                        dst = h1g_pb[par][:, ch * 128:(ch + 1) * 128]
                        src = h1g_d[ei, hi][:, ts_ * 128:(ts_ + 1) * 128]
                        if ch % 2 == 0:
                            st.dma(y, "dsy", dst, src)
                        else:
                            st.dma(s, "dsc", dst, src)
                h1mk[ts_] = (st.cnt["dsy"], st.cnt["dsc"])

            def load_tsub(ts_):
                load_x(ts_)
                load_h1(ts_)

            st.wait(y, "dsy", attn_scr_mk)
            load_tgroup(0)
            load_x(0)
            load_x(1)
            st.wait(y, "dsy", scr_done)
            st.wait(s, "dsy", scr_done)
            load_h1(0)
            load_h1(1)
            fr = {}
            for ts_ in range(16):
                tg = ts_ // 4
                par = ts_ % 3
                if ts_ + 2 < 16:
                    if ts_ >= 1:
                        y.wait_ge(SEM["pe"], fr[("mm", ts_ - 1)])
                        y.wait_ge(SEM["dve"], fr[("evac", (ts_ - 1) * 2 + 1)])
                        s.wait_ge(SEM["pe"], fr[("mm", ts_ - 1)])
                    if (ts_ + 2) % 4 == 0:
                        load_tgroup((ts_ + 2) // 4)
                    load_tsub(ts_ + 2)
                st.wait(t, "din", w2_ld)
                st.wait(t, "dsy", avmk[tg])
                opar = ts_ % 2
                ob = opar * 1024
                if ("mm", ts_ - 2) in fr:
                    st.wait(t, "dve", fr[("evac", (ts_ - 2) * 2 + 1)])
                for j in range(NJ):
                    lh = avf_pb[tg % 2][:, j * 512 + (ts_ % 4) * 128:
                                        j * 512 + (ts_ % 4 + 1) * 128]
                    t.matmul(ps_o[:, ob:ob + 512], lh,
                             wot_sb[:, j * 1024: j * 1024 + 512],
                             start=(j == 0), stop=False)
                    t.matmul(ps_o[:, ob + 512:ob + 1024], lh,
                             wot_sb[:, j * 1024 + 512: j * 1024 + 1024],
                             start=(j == 0), stop=False)
                st.wait(t, "dsy", h1mk[ts_][0])
                st.wait(t, "dsc", h1mk[ts_][1])
                for ei in range(NE):
                    for hi in range(NHI):
                        ch = ei * 4 + hi
                        lh = h1g_pb[par][:, ch * 128:(ch + 1) * 128]
                        t.matmul(ps_o[:, ob:ob + 512], lh,
                                 w2_sb[:, ch * 1024: ch * 1024 + 512],
                                 start=False, stop=False)
                        t.matmul(ps_o[:, ob + 512:ob + 1024], lh,
                                 w2_sb[:, ch * 1024 + 512: ch * 1024 + 1024],
                                 start=False, stop=False)
                lh = gtb_sb[:, ts_ * 128:(ts_ + 1) * 128]
                t.matmul(ps_o[:, ob:ob + 512], lh, b2_sb[:, 0:512],
                         start=False, stop=True)
                mm = t.matmul(ps_o[:, ob + 512:ob + 1024], lh, b2_sb[:, 512:1024],
                              start=False, stop=True)
                fr[("mm", ts_)] = st.inc(mm, "pe")
                st.wait(v, "pe", fr[("mm", ts_)])
                st.wait(v, "dsy", xmk[ts_])
                for cc in range(2):
                    oi = (ts_ * 2 + cc) % 4
                    if ("odma", ts_ * 2 + cc - 4) in fr:
                        st.wait(v, "dsy", fr[("odma", ts_ * 2 + cc - 4)])
                    v.tensor_add(out_pb[oi][:, :],
                                 ps_o[:, ob + cc * 512:ob + (cc + 1) * 512],
                                 x_pb[par][:, cc * 512:(cc + 1) * 512])
                    fr[("evac", ts_ * 2 + cc)] = st.inc(v.last, "dve")
                    st.wait(y, "dve", fr[("evac", ts_ * 2 + cc)])
                    fr[("odma", ts_ * 2 + cc)] = st.dma(
                        y, "dsy",
                        outd[ts_ * 128:(ts_ + 1) * 128, cc * 512:(cc + 1) * 512],
                        out_pb[oi][:, :])

            y.wait_ge(SEM["dsy"], st.cnt["dsy"])
            g.wait_ge(SEM["dsy"], st.cnt["dsy"])

    return nc


_NC_CACHE = {}


def kernel(**inputs):
    x = np.ascontiguousarray(np.asarray(inputs["L0_attn_x"], dtype=np.float32))
    if "nc" not in _NC_CACHE:
        _NC_CACHE["nc"] = build()
    nc = _NC_CACHE["nc"]

    com = {
        "wq": np.ascontiguousarray(inputs["L0_attn_Wq"], dtype=np.float32),
        "wk": np.ascontiguousarray(inputs["L0_attn_Wk"], dtype=np.float32),
        "wv": np.ascontiguousarray(inputs["L0_attn_Wv"], dtype=np.float32),
        "wo": np.ascontiguousarray(inputs["L0_attn_Wo"], dtype=np.float32),
        "gw": np.ascontiguousarray(inputs["L0_ffn_gate_W"], dtype=np.float32),
        "w1": np.ascontiguousarray(inputs["L0_ffn_W1"], dtype=np.float32),
        "b1": np.ascontiguousarray(inputs["L0_ffn_b1"], dtype=np.float32),
        "w2": np.ascontiguousarray(inputs["L0_ffn_W2"], dtype=np.float32),
        "b2": np.ascontiguousarray(inputs["L0_ffn_b2"], dtype=np.float32),
        "ws1": np.ascontiguousarray(inputs["L0_ffn_Ws1"], dtype=np.float32),
        "bs1": np.ascontiguousarray(
            np.asarray(inputs["L0_ffn_bs1"], dtype=np.float32).reshape(1, HID)),
        "ws2": np.ascontiguousarray(inputs["L0_ffn_Ws2"], dtype=np.float32),
        "bs2": np.ascontiguousarray(
            np.asarray(inputs["L0_ffn_bs2"], dtype=np.float32).reshape(1, C)),
        "rs": np.ascontiguousarray(
            np.asarray(inputs["L0_ffn_route_scale"], dtype=np.float32).reshape(1, 1)),
    }
    in_maps = []
    for c in range(8):
        b, half = c // 2, c % 2
        m = dict(com)
        m["xf"] = np.ascontiguousarray(x[b])
        m["xo"] = np.ascontiguousarray(x[b][own_rows(half)])
        in_maps.append(m)

    res = run_bass_kernel_spmd(nc, in_maps, core_ids=list(range(8)))
    out = np.empty((B, T, C), dtype=np.float32)
    for c in range(8):
        b, half = c // 2, c % 2
        out[b][own_rows(half)] = res.results[c]["out"]
    return out


if __name__ == "__main__":
    import importlib.util as _ilu
    spec = _ilu.spec_from_file_location("_ref", "/root/problem/reference.py")
    _r = _ilu.module_from_spec(spec)
    spec.loader.exec_module(_r)
    ins = {k: np.asarray(v) for k, v in _r.setup_inputs().items()}
    got = kernel(**ins)
    want = np.asarray(_r.reference(**_r.setup_inputs()))
    rv = float(np.mean((got - want) ** 2) / np.mean(want ** 2))
    print("max abs err:", np.abs(got - want).max(), "rvr:", rv)
